# version 27
# baseline (speedup 1.0000x reference)
import numpy as np
import ml_dtypes
from contextlib import ExitStack

import concourse.bass as bass
import concourse.mybir as mybir
from concourse.bass_utils import run_bass_kernel_spmd

F32 = mybir.dt.float32
BF16 = mybir.dt.bfloat16
AF = mybir.ActivationFunctionType
OP = mybir.AluOpType
AX = mybir.AxisListType

NCORES = 8
TC = 2048
NT = 4
TW = 512
NCH = 16
L = 128
D = 1024
FC = 8
NEG = -1.0e30
BFD = ml_dtypes.bfloat16


ENGINES = ("sync", "tensor", "vector", "scalar", "gpsimd")
SERIAL = False


class Ser:

    def __init__(self):
        self.ops = []

    def op(self, eng, fn, inc=1):
        self.ops.append((eng, fn, inc, None))

    def dma(self, eng, fn):
        self.ops.append((eng, fn, 16, "d"))

    def cc(self, eng, fn):
        self.ops.append((eng, fn, 1, "c"))

    def _scan(self, nc, sb_names, ps_names, scan_sem):
        recs = [None] * len(self.ops)
        with nc.discard():
            with nc.Block("scanblk") as blk:
                def mk(name):
                    def body(e):
                        for idx, (eng, fn, inc, dl) in enumerate(self.ops):
                            if eng != name:
                                continue
                            n0 = nc.n_instructions()
                            ins = fn(e)
                            if dl:
                                ins.then_inc(scan_sem[dl], inc)
                            recs[idx] = (n0, nc.n_instructions())
                    return body
                for name in ENGINES:
                    getattr(blk, name)(mk(name))
        names = nc.inst_names()
        rw = []
        for idx, (n0, n1) in enumerate(recs):
            reads, writes = [], []
            for i in range(n0, n1):
                ins = nc.get_inst_by_name(names[i])
                for arg, dst in ((a, reads) for a in ins.ins):
                    self._acc(arg, dst, sb_names, ps_names)
                for arg in ins.outs:
                    self._acc(arg, writes, sb_names, ps_names)
            rw.append((reads, writes))
        return rw

    @staticmethod
    def _acc(arg, dst, sb_names, ps_names):
        m = getattr(arg, "memref", None)
        ap = getattr(arg, "ap", None)
        if m is None or ap is None:
            return
        off = int(arg.offset)
        dims = [(int(p[0]), int(p[1])) for p in ap]
        if m in ps_names:
            dst.append((m, 0, 1 << 40))
            return
        if m in sb_names and len(dims) > 1:
            dims = dims[1:]
        span = sum(st * (ct - 1) for st, ct in dims)
        dst.append((m, off, off + span + 1))

    @staticmethod
    def _k(eng, dl):
        return eng + "_" + dl if dl else eng

    def build(self, nc, sems, sb_names, ps_names):
        rw = self._scan(nc, sb_names, ps_names,
                        {"d": sems["scan"], "c": sems["scan_c"]})
        nops = len(self.ops)
        cum = {}
        val = [0] * nops
        for i, (eng, fn, inc, dl) in enumerate(self.ops):
            k = self._k(eng, dl)
            cum[k] = cum.get(k, 0) + inc
            val[i] = cum[k]
        writers = {}
        readers = {}
        deps = [set() for _ in range(nops)]
        last_dma = {}
        for i in range(nops):
            eng_i, _, _, dl_i = self.ops[i]
            if dl_i:
                if eng_i in last_dma:
                    deps[i].add(last_dma[eng_i])
                last_dma[eng_i] = i
            reads, writes = rw[i]
            for (m, lo, hi) in reads:
                for (l2, h2, j) in writers.get(m, ()):
                    if lo < h2 and l2 < hi:
                        deps[i].add(j)
            for (m, lo, hi) in writes:
                for (l2, h2, j) in writers.get(m, ()):
                    if lo < h2 and l2 < hi:
                        deps[i].add(j)
                for (l2, h2, j) in readers.get(m, ()):
                    if lo < h2 and l2 < hi:
                        deps[i].add(j)
            for (m, lo, hi) in reads:
                readers.setdefault(m, []).append((lo, hi, i))
            for (m, lo, hi) in writes:
                writers.setdefault(m, []).append((lo, hi, i))
        if SERIAL:
            for i in range(1, nops):
                deps[i] = {i - 1}
        need = []
        for i, (eng, fn, inc, dl) in enumerate(self.ops):
            req = {}
            for j in deps[i]:
                pk = self._k(self.ops[j][0], self.ops[j][3])
                req[pk] = max(req.get(pk, 0), val[j])
            need.append(req)
        allk = sorted(cum)

        def mk(name):
            def body(e):
                water = {p: 0 for p in allk}
                for i, (eng, fn, inc, dl) in enumerate(self.ops):
                    if eng != name:
                        continue
                    for pk, v in need[i].items():
                        if water[pk] < v:
                            e.wait_ge(sems[pk], v)
                            water[pk] = v
                    fn(e).then_inc(sems[self._k(name, dl)], inc)
                if name == "sync":
                    for pk in allk:
                        if cum[pk] > water[pk] and pk != "sync_d":
                            e.wait_ge(sems[pk], cum[pk])
                    e.wait_ge(sems["sync_d"], cum["sync_d"])
            return body

        with nc.Block() as block:
            for name in ENGINES:
                getattr(block, name)(mk(name))


class _L:
    pass


def _lay(pairs):
    o = _L()
    c = 0
    for name, n in pairs:
        setattr(o, name, c)
        c += n
    o.n = c
    return o


FBL = _lay([("i128", 128), ("tri", 128), ("ones", 128), ("minfS", 128),
            ("minfI", 128), ("x0", 8), ("bq", 1), ("bk", 1), ("bv", 1),
            ("bg", 1), ("bo", 8), ("bab", 2), ("cm", 8), ("wabo", 32),
            ("rw", 64), ("oc1024", 1), ("oc1", 1), ("zero", 1), ("eps", 1), ("cmi", 8)])
BBL = _lay([("i128", 128), ("oc1", 1), ("oc1024", 1)])
RBL = _lay([("or1", 128)])
RFL = _lay([("or1", 128), ("eseg", 1024)])

RL_MU, RL_RH, RL_RL, RL_E1H, RL_E1L, RL_E2H, RL_E2L, RL_G = range(8)
RA_MU, RA_RH, RA_RL, RA_M2R = range(4)

(CL_LA, CL_LAL, CL_LNR, CL_BIASE, CL_X0R, CL_KTIL, CL_GL, CL_BETA,
 CL_LNB, CL_R, CL_RH, CL_RL, CL_T1, CL_T2, CL_T3, CL_T4) = [16 * i for i in range(16)]


def build_program(dbg=()):
    dbg = set(dbg)
    nc = bass.Bass("TRN2", target_bir_lowering=False)

    d_x = nc.dram_tensor("xT", [D, TC], F32, kind="ExternalInput").ap()
    d_wq = nc.dram_tensor("wqT", [D, 128], F32, kind="ExternalInput").ap()
    d_wk = nc.dram_tensor("wkT", [D, 128], F32, kind="ExternalInput").ap()
    d_wv = nc.dram_tensor("wvT", [D, 128], F32, kind="ExternalInput").ap()
    d_wg = nc.dram_tensor("wgT", [D, 128], F32, kind="ExternalInput").ap()
    d_wo = nc.dram_tensor("woT", [128, D], F32, kind="ExternalInput").ap()
    d_ld = nc.dram_tensor("ldT", [D, 128], F32, kind="ExternalInput").ap()
    d_lu = nc.dram_tensor("luT", [128, D], F32, kind="ExternalInput").ap()
    d_fb = nc.dram_tensor("fblob", [128, FBL.n], F32, kind="ExternalInput").ap()
    d_bb = nc.dram_tensor("bblob", [128, BBL.n], BF16, kind="ExternalInput").ap()
    d_rb = nc.dram_tensor("rblob", [1, RBL.n], BF16, kind="ExternalInput").ap()
    d_rf = nc.dram_tensor("rfblob", [1, RFL.n], F32, kind="ExternalInput").ap()
    d_y = nc.dram_tensor("y", [D, TC], F32, kind="ExternalOutput").ap()
    d_pqloc = nc.dram_tensor("pqloc", [128, 512], BF16, kind="Internal").ap()
    d_pqall = nc.dram_tensor("pqall", [NCORES * 128, 512], BF16, kind="Internal",
                             addr_space="Shared").ap()
    d_g2loc = nc.dram_tensor("g2loc", [128, 512], F32, kind="Internal").ap()
    d_bari = nc.dram_tensor("bari", [128, 512], F32, kind="Internal").ap()
    d_baro = nc.dram_tensor("baro", [128, 512], F32, kind="Internal").ap()
    d_g2all = nc.dram_tensor("g2all", [NCORES * 128, 512], F32, kind="Internal",
                             addr_space="Shared").ap()
    d_rows4 = nc.dram_tensor("scr_rows4", [4, TC], F32, kind="Internal").ap()
    d_rows1 = nc.dram_tensor("scr_rows1", [1, TC], F32, kind="Internal").ap()
    d_combo = nc.dram_tensor("scr_combo", [128, 128], F32, kind="Internal").ap()
    d_wsr = nc.dram_tensor("scr_wsr", [8, 128], F32, kind="Internal").ap()

    d_dbg = {}
    for name, shape, dt in (("x1", [D, TC], F32), ("x2", [D, TC], F32),
                            ("qf", [128, TC], BF16), ("kf", [128, TC], BF16),
                            ("vf", [128, TC], BF16), ("gf", [128, TC], BF16),
                            ("og", [128, TC], BF16), ("rl", [1, 8 * TC], BF16),
                            ("sX7", [128, NCH * 384], BF16), ("sE2", [128, NCH * 128], BF16),
                            ("sKn", [128, NCH * 128], BF16), ("sWf", [128, NCH * 128], BF16),
                            ("cl", [128, 256], F32), ("Rf", [128, 256], F32),
                            ("pqa", [NCORES * 128, 256], F32),
                            ("Pc", [128, 256], F32), ("Qc", [128, 256], F32),
                            ("cps", [128, 512], BF16)):
        if name in dbg:
            d_dbg[name] = nc.dram_tensor("dbg_" + name, shape, dt,
                                         kind="ExternalOutput").ap()

    ser = Ser()
    ctx = ExitStack()
    sems = {e: ctx.enter_context(nc.semaphore("S_" + e)) for e in ENGINES}
    for e in ENGINES:
        sems[e + "_d"] = ctx.enter_context(nc.semaphore("SD_" + e))
    sems["gpsimd_c"] = ctx.enter_context(nc.semaphore("SC_gpsimd"))
    sems["scan"] = ctx.enter_context(nc.semaphore("S_scan"))
    sems["scan_c"] = ctx.enter_context(nc.semaphore("S_scan_c"))
    sb_names, ps_names = set(), set()

    def sb(name, shape, dtype):
        sb_names.add(name)
        return ctx.enter_context(nc.sbuf_tensor(name, shape, dtype))

    def psum(name, shape, dtype):
        ps_names.add(name)
        return ctx.enter_context(nc.psum_tensor(name, shape, dtype))

    xm = sb("xm", [128, FC * TC], F32)
    qf = sb("qf", [128, TC], BF16)
    kf = sb("kf", [128, TC], BF16)
    vf = sb("vf", [128, TC], BF16)
    gf = sb("gf", [128, TC], BF16)
    dgog = sb("dgog", [128, TC], BF16)
    xbsc = sb("xbsc", [128, 2 * TW], BF16)
    sqs = sb("sqs", [128, 2 * TW], BF16)
    wqb = sb("wqb", [128, D], BF16)
    wkb = sb("wkb", [128, D], BF16)
    wvb = sb("wvb", [128, D], BF16)
    wgb = sb("wgb", [128, D], BF16)
    wob = sb("wob", [128, D], BF16)
    ldb = sb("ldb", [128, D], BF16)
    lub = sb("lub", [128, D], BF16)
    fb = sb("fb", [128, FBL.n], F32)
    bbs = sb("bbs", [128, BBL.n], BF16)
    rbs = sb("rbs", [1, RBL.n], BF16)
    rfs = sb("rfs", [1, RFL.n], F32)
    wabob = sb("wabob", [128, 32], BF16)
    sKn = sb("sKn", [128, NCH * 128], BF16)
    sWf = sb("sWf", [128, NCH * 128], BF16)
    sE2 = sb("sE2", [128, NCH * 128], BF16)
    sX7 = sb("sX7", [128, NCH * 384], BF16)
    Xa = sb("Xa", [128, 384], BF16)
    Xb = sb("Xb", [128, 384], BF16)
    cps = sb("cps", [128, 4 * 128], BF16)
    scrA = sb("scrA", [128, 128], BF16)
    scrB = sb("scrB", [128, 128], BF16)
    scrD = sb("scrD", [128, 128], BF16)
    sRj = sb("sRj", [128, NCH * 128], BF16)
    crs = sb("crs", [128, 6 * 128], BF16)
    Pc = sb("Pc", [128, 2 * 128], F32)
    Qc = sb("Qc", [128, 2 * 128], F32)
    Pcb = sb("Pcb", [128, 2 * 128], BF16)
    Qcb = sb("Qcb", [128, 2 * 128], BF16)
    Rf = sb("Rf", [128, 2 * 128], F32)
    PQin = sb("PQin", [128, 256], BF16)
    rowsA = sb("rowsA", [4, TC], F32)
    rowsB = sb("rowsB", [1, TC], F32)
    colsAB = sb("colsAB", [128, 64], F32)
    colsEx = None
    colsN2 = None
    cl = sb("cl", [128, 256], F32)
    combo = sb("combo", [128, 128], F32)
    rland = sb("rland", [1, 8 * TC], BF16)
    wsum8 = sb("wsum8", [128, 8], F32)
    wsrows = sb("wsrows", [1, 1024], BF16)
    rbcS = sb("rbcS", [128, 2 * TW], F32)
    xr = sb("xr", [128, TW], BF16)
    t0f = sb("t0f", [128, 64], F32)
    t0r = sb("t0r", [1, 64], F32)
    gcol = sb("gcol", [128, 4], F32)

    pA = psum("pA", [128, TW], F32)
    pB = psum("pB", [128, TW], F32)
    pC = psum("pC", [128, TW], F32)
    pS1 = psum("pS1", [128, TW], F32)
    pS2 = psum("pS2", [128, TW], F32)
    pS3 = psum("pS3", [128, TW], F32)
    pD = psum("pD", [128, TW], F32)
    pT = psum("pT", [128, 2 * TW], BF16)

    T0_SQ, T0_LN, T0_X2 = 0, 8, 24
    T0_MU, T0_R, T0_LN3 = 46, 47, 48

    def V(fn):
        ser.op("vector", fn)

    def Sc(fn):
        ser.op("scalar", fn)

    def Tn(fn):
        ser.op("tensor", fn)

    def G(fn):
        ser.op("gpsimd", fn)

    def fcol(off, n=1):
        return fb[0:128, off:off + n]

    ident = fb[0:128, FBL.i128:FBL.i128 + 128]
    or1b = rbs[0:1, RBL.or1:RBL.or1 + 128]
    or1f = rfs[0:1, RFL.or1:RFL.or1 + 128]
    ib128 = bbs[0:128, BBL.i128:BBL.i128 + 128]
    oc1b = bbs[0:128, BBL.oc1:BBL.oc1 + 1]
    oc1024b = bbs[0:128, BBL.oc1024:BBL.oc1024 + 1]

    def clc(off, n=16):
        return cl[0:128, off:off + n]

    def clj(off, j):
        return cl[0:128, off + j:off + j + 1]

    def xms(ci, t):
        return xm[:, ci * TC + t * TW: ci * TC + (t + 1) * TW]

    def rls(q, t):
        return rland[0:1, q * TC + t * TW: q * TC + (t + 1) * TW]

    def rlj(q, j):
        return rland[0:1, q * TC + j * 128: q * TC + (j + 1) * 128]

    ser.dma("sync", lambda e: e.dma_start(fb[:, :], d_fb))
    ser.dma("sync", lambda e: e.dma_start(bbs[:, :], d_bb))
    ser.dma("sync", lambda e: e.dma_start(rbs[:, :], d_rb))
    ser.dma("sync", lambda e: e.dma_start(rfs[:, :], d_rf))
    for ci in range(FC):
        ser.dma("sync" if ci % 2 == 0 else "scalar", lambda e, ci=ci: e.dma_start(
            xm[:, ci * TC:(ci + 1) * TC], d_x[ci * 128:(ci + 1) * 128, 0:TC]))
    for dst, src, rearr in ((wqb, d_wq, True), (wkb, d_wk, True), (wvb, d_wv, True),
                            (wgb, d_wg, True), (ldb, d_ld, True),
                            (wob, d_wo, False), (lub, d_lu, False)):
        if rearr:
            ser.dma("gpsimd", lambda e, dst=dst, src=src: e.dma_start(
                dst[:, :].rearrange("p (ci c) -> p ci c", ci=FC),
                src.rearrange("(ci p) c -> p ci c", p=128)))
        else:
            ser.dma("gpsimd", lambda e, dst=dst, src=src: e.dma_start(dst[:, :], src))
    V(lambda e: e.tensor_copy(wabob[:, :], fb[:, FBL.wabo:FBL.wabo + 32]))
    V(lambda e: e.memset(Qc[:, 0:128], 0.0))
    V(lambda e: e.memset(rbcS[:, 0:TW], 0.0))
    ser.dma("gpsimd", lambda e: e.dma_start(d_pqloc[0:128, 0:512], rbcS[:, 0:TW]))
    ser.dma("sync", lambda e: e.dma_start(d_g2loc[0:128, 0:512], rbcS[:, 0:TW]))
    ser.dma("sync", lambda e: e.dma_start(d_bari[0:128, 0:512], rbcS[:, 0:TW]))

    for i, w in enumerate((wqb, wkb, wvb, wgb, ldb)):
        def f(e, w=w):
            for ci in range(FC):
                ins = e.matmul(pS1[0:128, 0:1], w[:, ci * 128:(ci + 1) * 128],
                               oc1b, start=(ci == 0), stop=(ci == 7))
            return ins
        Tn(f)
        V(lambda e, i=i: e.tensor_copy(wsum8[:, i:i + 1], pS1[0:128, 0:1]))
    Tn(lambda e: e.transpose(pS1[0:5, 0:128], wsum8[:, 0:5], ident))
    Sc(lambda e: e.mul(combo[0:5, 0:128], pS1[0:5, 0:128], -1.0))
    ser.dma("sync", lambda e: e.dma_start(d_wsr[0:5, 0:128], combo[0:5, 0:128]))
    ser.dma("gpsimd", lambda e: e.dma_start(
        wsrows[0:1, 0:640], d_wsr[0:5, 0:128].rearrange("p t -> (p t)")))

    def f_wsab(e):
        for ci in range(FC):
            ins = e.matmul(pS1[0:4, 0:1], wabob[:, 4 * ci:4 * ci + 4],
                           oc1b, start=(ci == 0), stop=(ci == 7))
        return ins
    Tn(f_wsab)
    V(lambda e: e.tensor_copy(t0f[0:4, T0_SQ:T0_SQ + 1], pS1[0:4, 0:1]))
    Tn(lambda e: e.transpose(pS2[0:1, 0:4], t0f[0:4, T0_SQ:T0_SQ + 1],
                             fb[0:4, FBL.i128:FBL.i128 + 4]))
    V(lambda e: e.tensor_copy(t0r[0:1, 17:21], pS2[0:1, 0:4]))
    Tn(lambda e: e.matmul(pS1[0:128, 0:1], or1f, t0r[0:1, 17:18], start=True, stop=True))
    V(lambda e: e.tensor_copy(clc(CL_T3, 1), pS1[0:128, 0:1]))
    Tn(lambda e: e.matmul(pS1[0:128, 0:1], or1f, t0r[0:1, 18:19], start=True, stop=True))
    V(lambda e: e.tensor_copy(clc(CL_T4, 1), pS1[0:128, 0:1]))

    def t0_lnstats(src, dst):
        Tn(lambda e: e.matmul(pS1[0:1, 0:8], fcol(FBL.oc1), src, start=True, stop=True))
        V(lambda e: e.reduce_sum(t0r[0:1, 12:13], pS1[0:1, 0:8], axis=AX.X))
        Sc(lambda e: e.activation(t0r[0:1, 8:9], t0r[0:1, 12:13], AF.Copy, scale=1.0 / 1024.0))
        Sc(lambda e: e.activation(t0f[:, T0_SQ:T0_SQ + 8], src, AF.Square))
        Tn(lambda e: e.matmul(pS1[0:1, 0:8], fcol(FBL.oc1), t0f[:, T0_SQ:T0_SQ + 8],
                              start=True, stop=True))
        V(lambda e: e.reduce_sum(t0r[0:1, 12:13], pS1[0:1, 0:8], axis=AX.X))
        V(lambda e: e.tensor_tensor(t0r[0:1, 9:10], t0r[0:1, 8:9], t0r[0:1, 8:9], OP.mult))
        V(lambda e: e.tensor_scalar(t0r[0:1, 10:11], t0r[0:1, 12:13], 1.0 / 1024.0, None, OP.mult))
        V(lambda e: e.tensor_tensor(t0r[0:1, 10:11], t0r[0:1, 10:11], t0r[0:1, 9:10], OP.subtract))
        Sc(lambda e: e.activation(t0r[0:1, 11:12], t0r[0:1, 10:11], AF.Ln, bias=fb[0:1, FBL.eps:FBL.eps + 1]))
        Sc(lambda e: e.activation(t0r[0:1, 11:12], t0r[0:1, 11:12], AF.Exp, scale=-0.5))
        Tn(lambda e: e.matmul(pS1[0:128, 0:1], or1f, t0r[0:1, 8:9], start=True, stop=True))
        V(lambda e: e.tensor_copy(t0f[:, T0_MU:T0_MU + 1], pS1[0:128, 0:1]))
        Tn(lambda e: e.matmul(pS1[0:128, 0:1], or1f, t0r[0:1, 11:12], start=True, stop=True))
        V(lambda e: e.tensor_copy(t0f[:, T0_R:T0_R + 1], pS1[0:128, 0:1]))
        V(lambda e: e.tensor_scalar(t0f[:, dst:dst + 8], src,
                                    t0f[:, T0_MU:T0_MU + 1], None, OP.subtract))
        V(lambda e: e.tensor_scalar(t0f[:, dst:dst + 8], t0f[:, dst:dst + 8],
                                    t0f[:, T0_R:T0_R + 1], None, OP.mult))

    def t0_gates(ln_col, gi):
        def f(e):
            for ci in range(FC):
                ins = e.matmul(pS1[0:1, 0:8], t0f[:, ln_col + ci:ln_col + ci + 1],
                               fb[:, FBL.rw + ci * 8:FBL.rw + ci * 8 + 8],
                               start=(ci == 0), stop=(ci == 7))
            return ins
        Tn(f)
        V(lambda e: e.tensor_copy(t0r[0:1, 0:8], pS1[0:1, 0:8]))
        V(lambda e: e.reduce_max(t0r[0:1, 8:9], t0r[0:1, 0:8], axis=AX.X))
        V(lambda e: e.tensor_scalar(t0r[0:1, 0:8], t0r[0:1, 0:8], t0r[0:1, 8:9],
                                    None, OP.subtract))
        V(lambda e: e.tensor_scalar(t0r[0:1, 21:29], t0r[0:1, 0:8], 0.0, None, OP.is_ge))
        V(lambda e: e.scalar_tensor_tensor(t0r[0:1, 21:29], t0r[0:1, 21:29], NEG,
                                           t0r[0:1, 0:8], OP.mult, OP.add))
        V(lambda e: e.reduce_max(t0r[0:1, 9:10], t0r[0:1, 21:29], axis=AX.X))
        V(lambda e: e.tensor_scalar(t0r[0:1, 29:37], t0r[0:1, 0:8], t0r[0:1, 9:10],
                                    None, OP.is_ge))
        Sc(lambda e: e.activation(t0r[0:1, 37:45], t0r[0:1, 0:8], AF.Exp))
        V(lambda e: e.tensor_tensor(t0r[0:1, 37:45], t0r[0:1, 37:45], t0r[0:1, 29:37], OP.mult))
        V(lambda e: e.reduce_sum(t0r[0:1, 10:11], t0r[0:1, 37:45], axis=AX.X))
        V(lambda e: e.reciprocal(t0r[0:1, 11:12], t0r[0:1, 10:11]))
        V(lambda e: e.tensor_scalar(t0r[0:1, 37:45], t0r[0:1, 37:45], t0r[0:1, 11:12],
                                    None, OP.mult))
        def f2(e):
            for ee in range(8):
                ins = e.matmul(pS1[0:128, 0:1],
                               rfs[0:1, RFL.eseg + ee * 128:RFL.eseg + (ee + 1) * 128],
                               t0r[0:1, 37 + ee:38 + ee], start=(ee == 0), stop=(ee == 7))
            return ins
        Tn(f2)
        V(lambda e, gi=gi: e.tensor_copy(gcol[:, gi:gi + 1], pS1[0:128, 0:1]))
        V(lambda e, gi=gi: e.tensor_scalar(gcol[:, gi + 1:gi + 2], gcol[:, gi:gi + 1],
                                           0.5, None, OP.mult))

    t0_lnstats(fb[:, FBL.x0:FBL.x0 + 8], T0_LN)
    t0_gates(T0_LN, 0)

    def emit_row_to_cols(rows_ap, dram, nq, dst):
        ser.dma("sync", lambda e: e.dma_start(dram[0:nq, 0:TC], rows_ap))
        ser.dma("sync", lambda e: e.dma_start(
            combo[0:nq * 16, 0:128],
            dram[0:nq, 0:TC].rearrange("q (j t) -> (q j) t", t=128)))
        Tn(lambda e: e.transpose(pS1[0:128, 0:nq * 16], combo[0:nq * 16, 0:128],
                                 fb[0:nq * 16, FBL.i128:FBL.i128 + nq * 16]))
        V(lambda e: e.tensor_copy(dst[0:128, 0:nq * 16], pS1[0:128, 0:nq * 16]))

    def emit_cols_to_rows(src_ap, nq, q0):
        Tn(lambda e: e.transpose(pS1[0:nq * 16, 0:128], src_ap, ident))
        V(lambda e: e.tensor_copy(combo[0:nq * 16, 0:128], pS1[0:nq * 16, 0:128]))
        ser.dma("sync", lambda e: e.dma_start(d_combo[0:nq * 16, 0:128],
                                              combo[0:nq * 16, 0:128]))
        ser.dma("gpsimd", lambda e: e.dma_start(
            rland[0:1, q0 * TC:(q0 + nq) * TC],
            d_combo[0:nq * 16, 0:128].rearrange("p t -> (p t)")))

    def emit_r_from_muex(mu_ap, ex_ap):
        V(lambda e: e.tensor_tensor(clc(CL_T1), mu_ap, mu_ap, OP.mult))
        V(lambda e: e.tensor_tensor(clc(CL_T1), ex_ap, clc(CL_T1), OP.subtract))
        Sc(lambda e: e.activation(clc(CL_R), clc(CL_T1), AF.Ln, bias=fcol(FBL.eps)))
        Sc(lambda e: e.activation(clc(CL_R), clc(CL_R), AF.Exp, scale=-0.5))
        V(lambda e: e.tensor_copy(scrA[:, 0:16], clc(CL_R)))
        V(lambda e: e.tensor_copy(clc(CL_RH), scrA[:, 0:16]))
        V(lambda e: e.tensor_tensor(clc(CL_RL), clc(CL_R), clc(CL_RH), OP.subtract))

    def rbs_t(t):
        return rbcS[:, (t % 2) * TW:(t % 2 + 1) * TW]

    def emit_rbc(t):
        def f(e, t=t):
            e.matmul(pB[0:128, 0:TW], or1b, rls(RL_RH, t), start=True, stop=False)
            return e.matmul(pB[0:128, 0:TW], or1b, rls(RL_RL, t), start=False, stop=True)
        Tn(f)
        Sc(lambda e, t=t: e.copy(rbs_t(t), pB[0:128, 0:TW]))

    def stats_AD_inline():
        for t in range(NT):
            for ci in range(FC):
                xs = xbsc[:, (ci % 2) * TW:(ci % 2 + 1) * TW]
                sq_ = sqs[:, (ci % 2) * TW:(ci % 2 + 1) * TW]
                if ci % 2 == 0:
                    V(lambda e, t=t, ci=ci, xs=xs: e.tensor_copy(xs, xms(ci, t)))
                else:
                    Sc(lambda e, t=t, ci=ci, xs=xs: e.copy(xs, xms(ci, t)))
                Tn(lambda e, ci=ci, xs=xs: e.matmul(pC[0:1, 0:TW], oc1024b, xs,
                                                    start=(ci == 0), stop=(ci == 7)))
                V(lambda e, xs=xs, sq_=sq_: e.tensor_tensor(sq_, xs, xs, OP.mult))
                Tn(lambda e, ci=ci, sq_=sq_: e.matmul(pB[0:1, 0:TW], oc1024b, sq_,
                                                      start=(ci == 0), stop=(ci == 7)))
            Sc(lambda e, t=t: e.copy(rowsB[0:1, t * TW:(t + 1) * TW], pC[0:1, 0:TW]))
            V(lambda e, t=t: e.tensor_copy(rowsA[0:1, t * TW:(t + 1) * TW], pB[0:1, 0:TW]))
        emit_row_to_cols(rowsB[0:1, 0:TC], d_rows1, 1, colsAB[0:128, 48:64])
        V(lambda e: e.tensor_copy(clc(CL_T2), colsAB[:, 48:64]))
        emit_row_to_cols(rowsA[0:1, 0:TC], d_rows1, 1, colsAB[0:128, 48:64])
        emit_r_from_muex(clc(CL_T2), colsAB[:, 48:64])
        V(lambda e: e.scalar_tensor_tensor(clc(CL_T3), clc(CL_T2), -0.5, clc(CL_R),
                                           OP.mult, OP.mult))
        V(lambda e: e.tensor_copy(combo[:, 0:16], clc(CL_T2)))
        V(lambda e: e.tensor_copy(combo[:, 16:32], clc(CL_RH)))
        V(lambda e: e.tensor_copy(combo[:, 32:48], clc(CL_RL)))
        V(lambda e: e.tensor_copy(combo[:, 48:64], clc(CL_T3)))
        emit_cols_to_rows(combo[:, 0:64], 4, 0)

    def emit_ffn(ghalf_col):
        def emit_down(t):
            pDn = pA if t % 2 == 0 else pC
            emit_rbc(t)
            def fdown(e, t=t, pDn=pDn):
                return e.matmul(pDn[0:128, 0:TW], wsrows[0:1, 512:640],
                                rls(RA_MU, t), start=False, stop=True)
            for ci in range(FC):
                xs = xbsc[:, (ci % 2) * TW:(ci % 2 + 1) * TW]
                if ci % 2 == 0:
                    V(lambda e, t=t, ci=ci, xs=xs: e.tensor_copy(xs, xms(ci, t)))
                else:
                    Sc(lambda e, t=t, ci=ci, xs=xs: e.copy(xs, xms(ci, t)))
                Tn(lambda e, ci=ci, xs=xs, pDn=pDn: e.matmul(
                    pDn[0:128, 0:TW], ldb[:, ci * 128:(ci + 1) * 128],
                    xs, start=(ci == 0), stop=False))
            Tn(fdown)

        emit_down(0)
        for t in range(NT):
            pDn = pA if t % 2 == 0 else pC
            if t + 1 < NT:
                emit_down(t + 1)
            V(lambda e, t=t, pDn=pDn: e.scalar_tensor_tensor(
                dgog[:, t * TW:(t + 1) * TW], pDn[0:128, 0:TW],
                gcol[:, ghalf_col:ghalf_col + 1], rbs_t(t), OP.mult, OP.mult))
            def xrp_of(ci):
                return xr[:, :] if ci % 2 == 0 else sqs[:, 0:TW]

            def emit_xr(ci, t=t):
                V(lambda e, t=t, ci=ci: e.scalar_tensor_tensor(
                    xrp_of(ci), xms(ci, t), 0.5, rbs_t(t), OP.mult, OP.mult))
            emit_xr(0)
            emit_xr(1)
            for ci in range(FC):
                pU = (pS2, pS3, pS1, pD)[ci % 4]
                def fup(e, t=t, ci=ci, pU=pU):
                    e.matmul(pU[0:128, 0:TW], lub[:, ci * 128:(ci + 1) * 128],
                             dgog[:, t * TW:(t + 1) * TW], start=True, stop=False)
                    e.matmul(pU[0:128, 0:TW], or1b, rls(RA_M2R, t),
                             start=False, stop=False)
                    return e.matmul(pU[0:128, 0:TW], ib128, xrp_of(ci),
                                    start=False, stop=True)
                Tn(fup)
                if ci + 2 < FC:
                    emit_xr(ci + 2)
                V(lambda e, t=t, ci=ci, pU=pU: e.scalar_tensor_tensor(
                    xms(ci, t), pU[0:128, 0:TW], 0.0, xms(ci, t), OP.bypass, OP.add))

    stats_AD_inline()
    emit_ffn(1)

    if "x1" in d_dbg:
        for ci in range(FC):
            ser.dma("sync", lambda e, ci=ci: e.dma_start(
                d_dbg["x1"][ci * 128:(ci + 1) * 128, 0:TC], xm[:, ci * TC:(ci + 1) * TC]))

    for t in range(NT):
        for ci in range(FC):
            xs = xbsc[:, (ci % 2) * TW:(ci % 2 + 1) * TW]
            sq_ = sqs[:, (ci % 2) * TW:(ci % 2 + 1) * TW]
            if ci % 2 == 0:
                V(lambda e, t=t, ci=ci, xs=xs: e.tensor_copy(xs, xms(ci, t)))
            else:
                Sc(lambda e, t=t, ci=ci, xs=xs: e.copy(xs, xms(ci, t)))
            Tn(lambda e, ci=ci, xs=xs: e.matmul(pC[0:4, 0:TW], wabob[:, 4 * ci:4 * ci + 4],
                                                xs, start=(ci == 0), stop=(ci == 7)))
            V(lambda e, xs=xs, sq_=sq_: e.tensor_tensor(sq_, xs, xs, OP.mult))
            Tn(lambda e, ci=ci, sq_=sq_: e.matmul(pB[0:1, 0:TW], oc1024b, sq_,
                                                  start=(ci == 0), stop=(ci == 7)))
        Sc(lambda e, t=t: e.copy(rowsA[0:4, t * TW:(t + 1) * TW], pC[0:4, 0:TW]))
        V(lambda e, t=t: e.tensor_copy(rowsB[0:1, t * TW:(t + 1) * TW], pB[0:1, 0:TW]))
    emit_row_to_cols(rowsA[0:4, 0:TC], d_rows4, 4, colsAB)
    emit_row_to_cols(rowsB[0:1, 0:TC], d_rows1, 1, colsAB[0:128, 48:64])
    zaC = colsAB[:, 0:16]
    zbC = colsAB[:, 16:32]
    muC = colsAB[:, 32:48]
    emit_r_from_muex(muC, colsAB[:, 48:64])
    V(lambda e: e.tensor_copy(combo[:, 0:16], muC))
    V(lambda e: e.tensor_copy(combo[:, 16:32], clc(CL_RH)))
    V(lambda e: e.tensor_copy(combo[:, 32:48], clc(CL_RL)))
    emit_cols_to_rows(combo[:, 0:48], 3, RL_MU)

    for t in range(NT):
        emit_rbc(t)
        for ci in range(FC):
            xs = xbsc[:, (ci % 2) * TW:(ci % 2 + 1) * TW]
            if ci % 2 == 0:
                V(lambda e, t=t, ci=ci, xs=xs: e.tensor_copy(xs, xms(ci, t)))
            else:
                Sc(lambda e, t=t, ci=ci, xs=xs: e.copy(xs, xms(ci, t)))
            for pp, w in ((pA, wqb), (pS2, wkb), (pS3, wvb), (pC, wgb)):
                Tn(lambda e, pp=pp, w=w, ci=ci, xs=xs: e.matmul(
                    pp[0:128, 0:TW], w[:, ci * 128:(ci + 1) * 128], xs,
                    start=(ci == 0), stop=False))
        for pp, wsoff in ((pA, 0), (pS2, 128), (pS3, 256), (pC, 384)):
            Tn(lambda e, pp=pp, wsoff=wsoff, t=t: e.matmul(
                pp[0:128, 0:TW], wsrows[0:1, wsoff:wsoff + 128], rls(RL_MU, t),
                start=False, stop=True))
        for pi, (pp, dst, boff, sig) in enumerate(
                ((pA, qf, FBL.bq, False), (pS2, kf, FBL.bk, False),
                 (pS3, vf, FBL.bv, False), (pC, gf, FBL.bg, True))):
            scx = sqs[:, 0:TW] if pi % 2 == 0 else xr[:, :]
            V(lambda e, pp=pp, scx=scx, t=t: e.tensor_tensor(scx, pp[0:128, 0:TW],
                                                             rbs_t(t), OP.mult))
            if sig:
                Sc(lambda e, dst=dst, t=t, boff=boff, scx=scx: e.activation(
                    dst[:, t * TW:(t + 1) * TW], scx, AF.Sigmoid,
                    bias=fcol(boff)))
            else:
                V(lambda e, dst=dst, t=t, boff=boff, scx=scx: e.tensor_scalar(
                    dst[:, t * TW:(t + 1) * TW], scx, fcol(boff), None, OP.add))

    for name in ("qf", "kf", "vf", "gf"):
        if name in d_dbg:
            src = {"qf": qf, "kf": kf, "vf": vf, "gf": gf}[name]
            ser.dma("sync", lambda e, src=src, name=name: e.dma_start(
                d_dbg[name][0:128, 0:TC], src[:, 0:TC]))

    for t in range(NT):
        sq_ = sqs[:, (t % 2) * TW:(t % 2 + 1) * TW]
        V(lambda e, t=t, sq_=sq_: e.tensor_tensor(sq_, kf[:, t * TW:(t + 1) * TW],
                                                  kf[:, t * TW:(t + 1) * TW], OP.mult))
        Tn(lambda e, sq_=sq_, t=t: e.matmul(pB[0:1, 0:TW] if t % 2 == 0 else pC[0:1, 0:TW],
                                            oc1b, sq_, start=True, stop=True))
        Sc(lambda e, t=t: e.copy(rowsB[0:1, t * TW:(t + 1) * TW],
                                 pB[0:1, 0:TW] if t % 2 == 0 else pC[0:1, 0:TW]))
    emit_row_to_cols(rowsB[0:1, 0:TC], d_rows1, 1, colsAB[0:128, 48:64])

    def fold_z(zC, ws_off, bab_i, dst_off):
        V(lambda e: e.tensor_scalar(clc(CL_T1), muC, cl[0:128, ws_off:ws_off + 1],
                                    None, OP.mult))
        V(lambda e: e.tensor_tensor(clc(CL_T1), zC, clc(CL_T1), OP.subtract))
        V(lambda e: e.tensor_tensor(clc(CL_T1), clc(CL_T1), clc(CL_R), OP.mult))
        V(lambda e, dst_off=dst_off: e.tensor_scalar(
            cl[0:128, dst_off:dst_off + 16], clc(CL_T1),
            fb[0:128, FBL.bab + bab_i:FBL.bab + bab_i + 1], None, OP.add))

    fold_z(zbC, CL_T4, 1, CL_T2)
    Sc(lambda e: e.activation(clc(CL_BETA), clc(CL_T2), AF.Sigmoid))
    fold_z(zaC, CL_T3, 0, CL_T2)
    Sc(lambda e: e.activation(clc(CL_LNB), clc(CL_BETA), AF.Ln))
    Sc(lambda e: e.activation(clc(CL_T3), clc(CL_T2), AF.Relu, scale=-1.0))
    Sc(lambda e: e.activation(clc(CL_T4), clc(CL_T2), AF.Abs))
    Sc(lambda e: e.activation(clc(CL_T4), clc(CL_T4), AF.Exp, scale=-1.0))
    Sc(lambda e: e.activation(clc(CL_T4), clc(CL_T4), AF.Ln, bias=fcol(FBL.oc1)))
    V(lambda e: e.scalar_tensor_tensor(clc(CL_T2), clc(CL_T3), -1.0, clc(CL_T4),
                                       OP.mult, OP.subtract))
    Tn(lambda e: e.matmul(pS1[0:128, 0:16], fb[:, FBL.tri:FBL.tri + 128],
                          clc(CL_T2), start=True, stop=True))
    V(lambda e: e.tensor_copy(clc(CL_LA), pS1[0:128, 0:16]))
    Tn(lambda e: e.matmul(pS1[0:128, 0:16], fb[:, FBL.ones:FBL.ones + 128],
                          clc(CL_T2), start=True, stop=True))
    V(lambda e: e.tensor_copy(clc(CL_LAL), pS1[0:128, 0:16]))
    V(lambda e: e.tensor_scalar(clc(CL_T1), colsAB[:, 48:64], 1e-24, None, OP.max))
    Sc(lambda e: e.activation(clc(CL_LNR), clc(CL_T1), AF.Ln))
    V(lambda e: e.scalar_tensor_tensor(clc(CL_BIASE), clc(CL_LNR), -0.5, clc(CL_LA),
                                       OP.mult, OP.subtract))
    V(lambda e: e.tensor_tensor(clc(CL_BIASE), clc(CL_BIASE), clc(CL_LNB), OP.add))
    V(lambda e: e.scalar_tensor_tensor(clc(CL_T1), clc(CL_LNR), -0.5, clc(CL_LA),
                                       OP.mult, OP.add))
    Sc(lambda e: e.activation(clc(CL_X0R), clc(CL_T1), AF.Exp))
    V(lambda e: e.tensor_tensor(clc(CL_T2), clc(CL_LAL), clc(CL_BIASE), OP.add))
    Sc(lambda e: e.activation(clc(CL_KTIL), clc(CL_T2), AF.Exp))
    Sc(lambda e: e.activation(clc(CL_GL), clc(CL_LAL), AF.Exp))
    V(lambda e: e.tensor_copy(scrA[:, 0:16], clc(CL_T1)))
    V(lambda e: e.tensor_copy(clc(CL_T2), scrA[:, 0:16]))
    V(lambda e: e.tensor_copy(combo[:, 0:16], clc(CL_T2)))
    V(lambda e: e.tensor_tensor(combo[:, 16:32], clc(CL_T1), clc(CL_T2), OP.subtract))
    V(lambda e: e.tensor_copy(scrA[:, 0:16], clc(CL_LA)))
    V(lambda e: e.tensor_copy(clc(CL_T2), scrA[:, 0:16]))
    V(lambda e: e.tensor_copy(combo[:, 32:48], clc(CL_T2)))
    V(lambda e: e.tensor_tensor(combo[:, 48:64], clc(CL_LA), clc(CL_T2), OP.subtract))
    Sc(lambda e: e.activation(combo[:, 64:80], clc(CL_LA), AF.Exp))
    emit_cols_to_rows(combo[:, 0:80], 5, RL_E1H)
    if "rl" in d_dbg:
        ser.dma("sync", lambda e: e.dma_start(d_dbg["rl"][0:1, 0:8 * TC],
                                              rland[0:1, 0:8 * TC]))

    def kslice(s, j):
        return s[:, j * 128:(j + 1) * 128]

    def x7(j, part):
        return sX7[:, j * 384 + part * 128: j * 384 + (part + 1) * 128]

    dgA = dgog[:, 0:384]
    dgB = dgog[:, 384:768]
    dgCPS = dgog[:, 768:1280]
    dgE1 = dgog[:, 1280:1408]
    dgMn = dgog[:, 1408:1536]
    dgKt = dgog[:, 1536:1664]

    def eager_steps(j):
        p = j % 2
        XA = Xa if p == 0 else dgA
        XB = Xb if p == 0 else dgB
        CPS = cps if p == 0 else dgCPS
        E1 = scrA[:, :] if p == 0 else dgE1
        MN = scrB[:, :] if p == 0 else dgMn
        KT = scrD[:, :] if p == 0 else dgKt
        AE = combo[0:128, 0:128] if p == 0 else rbcS[0:128, 0:128]
        pX = pD if p == 0 else pA
        pV = pS1 if p == 0 else pB
        pG = pS2 if p == 0 else pC
        st = []
        def T(fn, glue=False): st.append(("tensor", fn, glue))
        def Vs(fn, glue=False): st.append(("vector", fn, glue))
        def Ss(fn, glue=False): st.append(("scalar", fn, glue))
        T(lambda e, j=j: e.transpose(pT[0:128, 0:128], kslice(kf, j), ib128), glue=True)
        Ss(lambda e, j=j: e.copy(kslice(sKn, j), pT[0:128, 0:128]))
        Vs(lambda e, j=j, XA=XA: e.tensor_scalar(XA[:, 128:256], kslice(sKn, j),
                                                 clj(CL_X0R, j), None, OP.mult))
        Vs(lambda e, j=j, KT=KT: e.tensor_scalar(KT, kslice(sKn, j),
                                                 clj(CL_KTIL, j), None, OP.mult))
        T(lambda e, j=j: e.transpose(pT[0:128, 0:128], kslice(vf, j), ib128), glue=True)
        Ss(lambda e, XA=XA: e.copy(XA[:, 0:128], pT[0:128, 0:128]))
        T(lambda e, j=j, pV=pV: e.matmul(pV[0:128, 0:128], kslice(vf, j), kslice(qf, j),
                                         start=True, stop=True))
        Vs(lambda e, XA=XA, pV=pV: e.tensor_copy(XA[:, 256:384], pV[0:128, 0:128]))
        T(lambda e, j=j, pG=pG: e.matmul(pG[0:128, 0:128], kslice(kf, j), kslice(kf, j),
                                         start=True, stop=True))
        def fbc1(e, j=j, pV=pV):
            e.matmul(pV[0:128, 0:128], or1b, rlj(RL_E1H, j), start=True, stop=False)
            return e.matmul(pV[0:128, 0:128], or1b, rlj(RL_E1L, j), start=False, stop=True)
        T(fbc1)
        Vs(lambda e, AE=AE, pV=pV: e.scalar_tensor_tensor(AE, pV[0:128, 0:128], 0.0,
                                                          fb[:, FBL.minfS:FBL.minfS + 128],
                                                          OP.bypass, OP.add))
        Ss(lambda e, j=j, E1=E1, AE=AE: e.activation(E1, AE, AF.Exp,
                                                     bias=clj(CL_BIASE, j)))
        Vs(lambda e, E1=E1, pG=pG, CPS=CPS: e.scalar_tensor_tensor(
            CPS[:, 0:128], pG[0:128, 0:128], -1.0, E1, OP.mult, OP.mult))
        T(lambda e, CPS=CPS: e.transpose(pT[0:128, 0:128], CPS[:, 0:128], ib128), glue=True)
        Ss(lambda e, CPS=CPS: e.copy(CPS[:, 128:256], pT[0:128, 0:128]))
        def fbc2(e, j=j, pV=pV):
            e.matmul(pV[0:128, 0:128], or1b, rlj(RL_E2H, j), start=True, stop=False)
            return e.matmul(pV[0:128, 0:128], or1b, rlj(RL_E2L, j), start=False, stop=True)
        T(fbc2)
        Vs(lambda e, AE=AE, pV=pV: e.scalar_tensor_tensor(AE, pV[0:128, 0:128], 0.0,
                                                          fb[:, FBL.minfI:FBL.minfI + 128],
                                                          OP.bypass, OP.add))
        Ss(lambda e, j=j, AE=AE: e.activation(kslice(sE2, j), AE, AF.Exp,
                                              bias=clj(CL_BIASE, j)))
        for k in range(7):
            cur = (k % 2) * 256
            nxt = 256 - cur
            Xcur = XA if k % 2 == 0 else XB
            Xnxt = XB if k % 2 == 0 else XA
            last = (k == 6)
            xdst = sX7[:, j * 384:(j + 1) * 384] if last else Xnxt[:, 0:384]
            T(lambda e, cur=cur, Xcur=Xcur, CPS=CPS, pX=pX: e.matmul(
                pX[0:128, 0:384], CPS[:, cur:cur + 128], Xcur[:, 0:384],
                start=True, stop=True))
            Vs(lambda e, Xcur=Xcur, xdst=xdst, pX=pX: e.scalar_tensor_tensor(
                xdst, pX[0:128, 0:384], 0.0, Xcur[:, 0:384], OP.bypass, OP.add))
            if not last:
                T(lambda e, cur=cur, CPS=CPS, pV=pV: e.matmul(
                    pV[0:128, 0:128], CPS[:, cur:cur + 128],
                    CPS[:, cur + 128:cur + 256], start=True, stop=True))
                Ss(lambda e, nxt=nxt, CPS=CPS, pV=pV: e.copy(
                    CPS[:, nxt + 128:nxt + 256], pV[0:128, 0:128]))
                T(lambda e, cur=cur, CPS=CPS, pG=pG: e.matmul(
                    pG[0:128, 0:128], CPS[:, cur + 128:cur + 256],
                    CPS[:, cur:cur + 128], start=True, stop=True))
                Vs(lambda e, nxt=nxt, CPS=CPS, pG=pG: e.tensor_copy(
                    CPS[:, nxt:nxt + 128], pG[0:128, 0:128]))
        T(lambda e, j=j: e.transpose(pT[0:128, 0:128], x7(j, 1), ib128), glue=True)
        Ss(lambda e, j=j: e.copy(kslice(sWf, j), pT[0:128, 0:128]))
        cur = (j % 2) * 128
        prv = 128 - cur
        pcb_old = ib128 if j == 0 else Pcb[:, prv:prv + 128]
        qcb_old = None if j == 0 else Qcb[:, prv:prv + 128]
        pcf_old = ident if j == 0 else Pc[:, prv:prv + 128]
        qcf_old = Qc[:, 0:128] if j == 0 else Qc[:, prv:prv + 128]
        T(lambda e, j=j, KT=KT: e.matmul(pS3[0:128, 0:128], x7(j, 1), KT,
                                         start=True, stop=True), glue=True)
        Ss(lambda e, MN=MN: e.mul(MN, pS3[0:128, 0:128], -1.0), glue=True)
        def f_q(e, j=j, qcb_old=qcb_old, KT=KT, MN=MN):
            ins = e.matmul(pS3[0:128, 0:128], KT, x7(j, 0), start=True,
                           stop=(j == 0))
            if j > 0:
                ins = e.matmul(pS3[0:128, 0:128], MN, qcb_old,
                               start=False, stop=True)
            return ins
        T(f_q, glue=True)
        Vs(lambda e, j=j, cur=cur, qcf_old=qcf_old: e.scalar_tensor_tensor(
            Qc[:, cur:cur + 128], qcf_old, clj(CL_GL, j), pS3[0:128, 0:128],
            OP.mult, OP.add), glue=True)
        T(lambda e, pcb_old=pcb_old, MN=MN: e.matmul(pS3[0:128, 0:128], MN, pcb_old,
                                                     start=True, stop=True), glue=True)
        Vs(lambda e, j=j, cur=cur, pcf_old=pcf_old: e.scalar_tensor_tensor(
            Pc[:, cur:cur + 128], pcf_old, clj(CL_GL, j), pS3[0:128, 0:128],
            OP.mult, OP.add), glue=True)
        Vs(lambda e, cur=cur: e.tensor_copy(Qcb[:, cur:cur + 128], Qc[:, cur:cur + 128]),
           glue=True)
        Vs(lambda e, cur=cur: e.tensor_copy(Pcb[:, cur:cur + 128], Pc[:, cur:cur + 128]))
        return st

    def emit_glued(lst, pos):
        while pos < len(lst):
            eng, fn, glue = lst[pos]
            ser.op(eng, fn)
            pos += 1
            if not glue:
                break
        return pos

    for j0 in range(0, NCH, 2):
        A = eager_steps(j0)
        B = eager_steps(j0 + 1)
        ia = ib = 0
        while ia < len(A) or ib < len(B):
            if ia < len(A):
                ia = emit_glued(A, ia)
            if ib < len(B):
                ib = emit_glued(B, ib)

    pq_last = ((NCH - 1) % 2) * 128

    Tn(lambda e: e.transpose(pS1[0:128, 0:128], Pc[:, pq_last:pq_last + 128], ident))
    V(lambda e: e.tensor_copy(combo[:, 0:128], pS1[0:128, 0:128]))
    ser.dma("gpsimd", lambda e: e.dma_start(d_pqloc[0:128, 0:128], combo[:, 0:128]))
    ser.dma("gpsimd", lambda e: e.dma_start(d_pqloc[0:128, 128:256],
                                            Qc[:, pq_last:pq_last + 128]))
    ser.cc("gpsimd", lambda e: e.collective_compute(
        "AllGather", OP.bypass, replica_groups=[list(range(NCORES))],
        ins=[d_pqloc], outs=[d_pqall]))
    for nm2, src2 in (("Pc", Pc), ("Qc", Qc)):
        if nm2 in d_dbg:
            ser.dma("sync", lambda e, nm2=nm2, src2=src2: e.dma_start(
                d_dbg[nm2][0:128, 0:256], src2[:, 0:256]))
    V(lambda e: e.memset(Rf[:, 0:128], 0.0))
    for j in range(NCORES - 1):
        pqs = dgog[:, j * 256:(j + 1) * 256]
        ser.dma("sync" if j % 2 == 0 else "scalar", lambda e, j=j, pqs=pqs:
                e.dma_start(pqs, d_pqall[j * 128:(j + 1) * 128, 0:256]))
        V(lambda e, j=j: e.tensor_scalar(scrB[:, :], ib128,
                                         fb[0:128, FBL.cmi + j:FBL.cmi + j + 1],
                                         None, OP.mult))
        V(lambda e, j=j, pqs=pqs: e.scalar_tensor_tensor(
            pqs[:, 0:128], pqs[:, 0:128], fb[0:128, FBL.cm + j:FBL.cm + j + 1],
            scrB[:, :], OP.mult, OP.add))
        V(lambda e, j=j, pqs=pqs: e.tensor_scalar(
            pqs[:, 128:256], pqs[:, 128:256],
            fb[0:128, FBL.cm + j:FBL.cm + j + 1], None, OP.mult))
    for j in range(NCORES - 1):
        pqs = dgog[:, j * 256:(j + 1) * 256]
        V(lambda e: e.tensor_copy(scrD[:, :], Rf[:, 0:128]))
        Tn(lambda e, pqs=pqs: e.matmul(pS1[0:128, 0:128], pqs[:, 0:128], scrD[:, :],
                                       start=True, stop=True))
        V(lambda e, pqs=pqs: e.scalar_tensor_tensor(Rf[:, 0:128], pS1[0:128, 0:128],
                                                    0.0, pqs[:, 128:256],
                                                    OP.bypass, OP.add))

    for j in range(NCH):
        rcur = (j % 2) * 128
        rnxt = 128 - rcur
        Sc(lambda e, j=j, rcur=rcur: e.copy(kslice(sRj, j), Rf[:, rcur:rcur + 128]))
        Sc(lambda e, j=j: e.activation(scrD[:, :], kslice(sKn, j), AF.Copy,
                                       scale=clj(CL_KTIL, j)))
        Tn(lambda e, j=j: e.matmul(pD[0:128, 0:128], kslice(sWf, j), kslice(sRj, j),
                                   start=True, stop=True))
        V(lambda e, j=j: e.scalar_tensor_tensor(scrA[:, :], x7(j, 0), 0.0,
                                                pD[0:128, 0:128], OP.bypass, OP.subtract))
        Tn(lambda e, j=j: e.matmul(pD[0:128, 128:256], scrD[:, :], scrA[:, :],
                                   start=True, stop=True))
        V(lambda e, j=j, rcur=rcur, rnxt=rnxt: e.scalar_tensor_tensor(
            Rf[:, rnxt:rnxt + 128], Rf[:, rcur:rcur + 128], clj(CL_GL, j),
            pD[0:128, 128:256], OP.mult, OP.add))
    def emit_wo_pair(t, pr, alt):
        for k in (0, 1):
            ci = 2 * pr + k
            pW = pD if k == 0 else alt
            Tn(lambda e, t=t, ci=ci, pW=pW: e.matmul(
                pW[0:128, 0:TW], wob[:, ci * 128:(ci + 1) * 128],
                dgog[:, t * TW:(t + 1) * TW], start=True, stop=True))
            V(lambda e, t=t, ci=ci, pW=pW: e.scalar_tensor_tensor(
                xms(ci, t), pW[0:128, 0:TW],
                fb[0:128, FBL.bo + ci:FBL.bo + ci + 1], xms(ci, t), OP.add, OP.add))

    def emit_gates2():
        V(lambda e: e.tensor_copy(t0f[:, T0_X2:T0_X2 + 8],
                                  xm[0:128, 0:FC * TC:TC]))
        ser.dma("sync", lambda e: e.dma_start(d_g2loc[0:128, 0:8],
                                              t0f[:, T0_X2:T0_X2 + 8]))
        ser.cc("gpsimd", lambda e: e.collective_compute(
            "AllGather", OP.bypass, replica_groups=[list(range(NCORES))],
            ins=[d_g2loc], outs=[d_g2all]))
        ser.cc("gpsimd", lambda e: e.collective_compute(
            "AllReduce", OP.add, replica_groups=[list(range(NCORES))],
            ins=[d_bari], outs=[d_baro]))

    for j in range(NCH):
        p = j % 2
        pRQ = pA if p == 0 else pS1
        pS2_ = pB if p == 0 else pS2
        pO = pC if p == 0 else pS3
        cRt = crs[:, (0 + p) * 128:(1 + p) * 128]
        cRQ = crs[:, (2 + p) * 128:(3 + p) * 128]
        cRG = crs[:, (4 + p) * 128:(5 + p) * 128]
        Tn(lambda e, j=j: e.transpose(pT[0:128, 0:128], kslice(sRj, j), ib128))
        Sc(lambda e, cRt=cRt: e.copy(cRt[:, :], pT[0:128, 0:128]))
        Tn(lambda e, j=j, cRt=cRt, pRQ=pRQ: e.matmul(pRQ[0:128, 0:128], cRt[:, :],
                                                     kslice(qf, j), start=True, stop=True))
        Sc(lambda e, cRQ=cRQ, pRQ=pRQ: e.copy(cRQ[:, :], pRQ[0:128, 0:128]))
        Tn(lambda e, j=j, pRQ=pRQ: e.matmul(pRQ[0:128, 0:128], or1b, rlj(RL_G, j),
                                            start=True, stop=True))
        V(lambda e, cRG=cRG, cRQ=cRQ, pRQ=pRQ: e.tensor_tensor(
            cRG[:, :], cRQ[:, :], pRQ[0:128, 0:128], OP.mult))
        Tn(lambda e, j=j, cRQ=cRQ, pS2_=pS2_: e.matmul(pS2_[0:128, 0:128], kslice(sWf, j),
                                                       cRQ[:, :], start=True, stop=True))
        V(lambda e, j=j, cRQ=cRQ, pS2_=pS2_: e.scalar_tensor_tensor(
            cRQ[:, :], x7(j, 2), 0.0, pS2_[0:128, 0:128], OP.bypass, OP.subtract))
        V(lambda e, j=j, cRQ=cRQ: e.tensor_tensor(cRQ[:, :], cRQ[:, :],
                                                  kslice(sE2, j), OP.mult))
        def f_o(e, j=j, cRQ=cRQ, cRG=cRG, pO=pO):
            e.matmul(pO[0:128, 0:128], kslice(sKn, j), cRQ[:, :], start=True, stop=False)
            return e.matmul(pO[0:128, 0:128], ib128, cRG[:, :], start=False, stop=True)
        Tn(f_o)
        V(lambda e, j=j, pO=pO: e.tensor_tensor(kslice(dgog, j), pO[0:128, 0:128],
                                                kslice(gf, j), OP.mult))
        if j >= 3:
            emit_wo_pair((j - 3) // 4, (j - 3) % 4, pS2 if p == 0 else pB)
        if j == 6:
            emit_gates2()
    for pr in (1, 2, 3):
        emit_wo_pair(3, pr, pS2 if pr % 2 == 1 else pB)

    if "og" in d_dbg:
        ser.dma("sync", lambda e: e.dma_start(d_dbg["og"][0:128, 0:TC], dgog[:, 0:TC]))
    for nm, src in (("sX7", sX7), ("sE2", sE2), ("sKn", sKn), ("sWf", sWf),
                    ("cl", cl), ("Rf", Rf)):
        if nm in d_dbg:
            ser.dma("sync", lambda e, nm=nm, src=src: e.dma_start(
                d_dbg[nm][0:128, :], src[:, :]))

    if "x2" in d_dbg:
        for ci in range(FC):
            ser.dma("sync", lambda e, ci=ci: e.dma_start(
                d_dbg["x2"][ci * 128:(ci + 1) * 128, 0:TC], xm[:, ci * TC:(ci + 1) * TC]))

    stats_AD_inline()
    ser.dma("sync", lambda e: e.dma_start(t0f[:, T0_X2:T0_X2 + 8], d_g2all[0:128, 0:8]))
    t0_lnstats(t0f[:, T0_X2:T0_X2 + 8], T0_LN3)
    t0_gates(T0_LN3, 2)
    emit_ffn(3)

    for ci in range(FC):
        ser.dma("sync" if ci % 2 == 0 else "scalar", lambda e, ci=ci: e.dma_start(
            d_y[ci * 128:(ci + 1) * 128, 0:TC], xm[:, ci * TC:(ci + 1) * TC]))

    ser.build(nc, sems, sb_names, ps_names)
    ctx.close()
    return nc


def _host_blobs(inputs, core):
    eye = np.eye(128, dtype=np.float32)
    tri = np.triu(np.ones((128, 128), dtype=np.float32))
    ones = np.ones((128, 128), dtype=np.float32)
    ss, tt = np.meshgrid(np.arange(128), np.arange(128), indexing="ij")
    minfS = np.where(ss < tt, 0.0, NEG).astype(np.float32)
    minfI = np.where(ss <= tt, 0.0, NEG).astype(np.float32)
    x = np.asarray(inputs["L0_attn_x"][0], np.float32)
    x0fm = np.ascontiguousarray(x[0].reshape(8, 128).T)
    bq = np.asarray(inputs["L0_attn_bq"], np.float32).reshape(128, 1)
    bk = np.asarray(inputs["L0_attn_bk"], np.float32).reshape(128, 1)
    bv = np.asarray(inputs["L0_attn_bv"], np.float32).reshape(128, 1)
    bg = np.asarray(inputs["L0_attn_bg"], np.float32).reshape(128, 1)
    bo = np.ascontiguousarray(np.asarray(inputs["L0_attn_bo"], np.float32).reshape(8, 128).T)
    ba = float(np.asarray(inputs["L0_attn_ba"]).reshape(-1)[0])
    bb_ = float(np.asarray(inputs["L0_attn_bb"]).reshape(-1)[0])
    bab = np.empty((128, 2), np.float32)
    bab[:, 0] = ba
    bab[:, 1] = bb_
    cm = np.zeros((128, 8), np.float32)
    cm[:, :core] = 1.0
    WaT = np.asarray(inputs["L0_attn_Wa"], np.float32).T
    WbT = np.asarray(inputs["L0_attn_Wb"], np.float32).T
    wabo = np.zeros((128, 32), np.float32)
    for ci in range(8):
        wabo[:, 4 * ci] = WaT[ci * 128:(ci + 1) * 128, 0]
        wabo[:, 4 * ci + 1] = WbT[ci * 128:(ci + 1) * 128, 0]
        wabo[:, 4 * ci + 2] = 1.0 / 1024.0
    rwT = np.asarray(inputs["L0_ffn_router_w"], np.float32).T
    rw = np.zeros((128, 64), np.float32)
    for ci in range(8):
        rw[:, ci * 8:(ci + 1) * 8] = rwT[ci * 128:(ci + 1) * 128, :]
    oc1024 = np.full((128, 1), 1.0 / 1024.0, np.float32)
    oc1 = np.ones((128, 1), np.float32)
    zero = np.zeros((128, 1), np.float32)
    eps = np.full((128, 1), 1e-5, np.float32)
    cmi = 1.0 - cm
    fblob = np.concatenate([eye, tri, ones, minfS, minfI, x0fm, bq, bk, bv, bg,
                            bo, bab, cm, wabo, rw, oc1024, oc1, zero, eps, cmi],
                           axis=1).astype(np.float32)
    bblob = np.concatenate([eye, oc1, oc1024], axis=1).astype(BFD)
    rblob = np.ones((1, 128), np.float32).astype(BFD)
    eseg = np.zeros((1, 1024), np.float32)
    for ee in range(8):
        eseg[0, ee * 128 + ee * 16:ee * 128 + (ee + 1) * 16] = 1.0
    rfblob = np.concatenate([np.ones((1, 128), np.float32), eseg], axis=1).astype(np.float32)
    return fblob, bblob, rblob, rfblob


_PROG_CACHE = {}


def _get_prog(dbg=()):
    key = tuple(sorted(dbg))
    if key not in _PROG_CACHE:
        _PROG_CACHE[key] = build_program(dbg)
    return _PROG_CACHE[key]


def kernel(dbg=(), **inputs):
    x = np.asarray(inputs["L0_attn_x"])
    assert x.shape == (1, 16384, 1024)
    xf = np.asarray(x[0], np.float32)
    wqT = np.ascontiguousarray(np.asarray(inputs["L0_attn_Wq"], np.float32).T)
    wkT = np.ascontiguousarray(np.asarray(inputs["L0_attn_Wk"], np.float32).T)
    wvT = np.ascontiguousarray(np.asarray(inputs["L0_attn_Wv"], np.float32).T)
    wgT = np.ascontiguousarray(np.asarray(inputs["L0_attn_Wg"], np.float32).T)
    woT = np.ascontiguousarray(np.asarray(inputs["L0_attn_Wo"], np.float32).T)
    ldT = np.ascontiguousarray(
        np.asarray(inputs["L0_ffn_lora_down"], np.float32).reshape(128, 1024).T)
    lu_cat = np.asarray(inputs["L0_ffn_lora_up"], np.float32).transpose(1, 0, 2).reshape(1024, 128)
    luT = np.ascontiguousarray(lu_cat.T)

    in_maps = []
    for c in range(NCORES):
        fblob, bblob, rblob, rfblob = _host_blobs(inputs, c)
        xT = np.ascontiguousarray(xf[c * TC:(c + 1) * TC, :].T)
        in_maps.append(dict(xT=xT, wqT=wqT, wkT=wkT, wvT=wvT, wgT=wgT, woT=woT,
                            ldT=ldT, luT=luT, fblob=fblob, bblob=bblob,
                            rblob=rblob, rfblob=rfblob))

    nc = _get_prog(dbg)
    res = run_bass_kernel_spmd(nc, in_maps, core_ids=list(range(NCORES)))
    out = np.empty((16384, 1024), np.float32)
    for c in range(NCORES):
        out[c * TC:(c + 1) * TC, :] = res.results[c]["y"].T
    if dbg:
        return out[None], res
    return out[None]


# version 28
# speedup vs baseline: 1.0423x; 1.0423x over previous
import numpy as np
import ml_dtypes
from contextlib import ExitStack

import concourse.bass as bass
import concourse.mybir as mybir
from concourse.bass_utils import run_bass_kernel_spmd

F32 = mybir.dt.float32
BF16 = mybir.dt.bfloat16
AF = mybir.ActivationFunctionType
OP = mybir.AluOpType
AX = mybir.AxisListType

NCORES = 8
TC = 2048
NT = 4
TW = 512
NCH = 16
L = 128
D = 1024
FC = 8
NEG = -1.0e30
BFD = ml_dtypes.bfloat16


ENGINES = ("sync", "tensor", "vector", "scalar", "gpsimd")
SERIAL = False


class Ser:

    def __init__(self):
        self.ops = []

    def op(self, eng, fn, inc=1):
        self.ops.append((eng, fn, inc, None))

    def dma(self, eng, fn):
        self.ops.append((eng, fn, 16, "d"))

    def cc(self, eng, fn):
        self.ops.append((eng, fn, 1, "c"))

    def _scan(self, nc, sb_names, ps_names, scan_sem):
        recs = [None] * len(self.ops)
        with nc.discard():
            with nc.Block("scanblk") as blk:
                def mk(name):
                    def body(e):
                        for idx, (eng, fn, inc, dl) in enumerate(self.ops):
                            if eng != name:
                                continue
                            n0 = nc.n_instructions()
                            ins = fn(e)
                            if dl:
                                ins.then_inc(scan_sem[dl], inc)
                            recs[idx] = (n0, nc.n_instructions())
                    return body
                for name in ENGINES:
                    getattr(blk, name)(mk(name))
        names = nc.inst_names()
        rw = []
        for idx, (n0, n1) in enumerate(recs):
            reads, writes = [], []
            for i in range(n0, n1):
                ins = nc.get_inst_by_name(names[i])
                for arg, dst in ((a, reads) for a in ins.ins):
                    self._acc(arg, dst, sb_names, ps_names)
                for arg in ins.outs:
                    self._acc(arg, writes, sb_names, ps_names)
            rw.append((reads, writes))
        return rw

    @staticmethod
    def _acc(arg, dst, sb_names, ps_names):
        m = getattr(arg, "memref", None)
        ap = getattr(arg, "ap", None)
        if m is None or ap is None:
            return
        off = int(arg.offset)
        dims = [(int(p[0]), int(p[1])) for p in ap]
        if m in ps_names:
            dst.append((m, 0, 1 << 40))
            return
        if m in sb_names and len(dims) > 1:
            dims = dims[1:]
        span = sum(st * (ct - 1) for st, ct in dims)
        dst.append((m, off, off + span + 1))

    @staticmethod
    def _k(eng, dl):
        return eng + "_" + dl if dl else eng

    def build(self, nc, sems, sb_names, ps_names):
        rw = self._scan(nc, sb_names, ps_names,
                        {"d": sems["scan"], "c": sems["scan_c"]})
        nops = len(self.ops)
        cum = {}
        val = [0] * nops
        for i, (eng, fn, inc, dl) in enumerate(self.ops):
            k = self._k(eng, dl)
            cum[k] = cum.get(k, 0) + inc
            val[i] = cum[k]
        writers = {}
        readers = {}
        deps = [set() for _ in range(nops)]
        last_dma = {}
        for i in range(nops):
            eng_i, _, _, dl_i = self.ops[i]
            if dl_i:
                if eng_i in last_dma:
                    deps[i].add(last_dma[eng_i])
                last_dma[eng_i] = i
            reads, writes = rw[i]
            for (m, lo, hi) in reads:
                for (l2, h2, j) in writers.get(m, ()):
                    if lo < h2 and l2 < hi:
                        deps[i].add(j)
            for (m, lo, hi) in writes:
                for (l2, h2, j) in writers.get(m, ()):
                    if lo < h2 and l2 < hi:
                        deps[i].add(j)
                for (l2, h2, j) in readers.get(m, ()):
                    if lo < h2 and l2 < hi:
                        deps[i].add(j)
            for (m, lo, hi) in reads:
                readers.setdefault(m, []).append((lo, hi, i))
            for (m, lo, hi) in writes:
                writers.setdefault(m, []).append((lo, hi, i))
        if SERIAL:
            for i in range(1, nops):
                deps[i] = {i - 1}
        need = []
        for i, (eng, fn, inc, dl) in enumerate(self.ops):
            req = {}
            for j in deps[i]:
                pk = self._k(self.ops[j][0], self.ops[j][3])
                req[pk] = max(req.get(pk, 0), val[j])
            need.append(req)
        allk = sorted(cum)

        def mk(name):
            def body(e):
                water = {p: 0 for p in allk}
                for i, (eng, fn, inc, dl) in enumerate(self.ops):
                    if eng != name:
                        continue
                    for pk, v in need[i].items():
                        if water[pk] < v:
                            e.wait_ge(sems[pk], v)
                            water[pk] = v
                    fn(e).then_inc(sems[self._k(name, dl)], inc)
                if name == "sync":
                    for pk in allk:
                        if cum[pk] > water[pk] and pk != "sync_d":
                            e.wait_ge(sems[pk], cum[pk])
                    e.wait_ge(sems["sync_d"], cum["sync_d"])
            return body

        with nc.Block() as block:
            for name in ENGINES:
                getattr(block, name)(mk(name))


class _L:
    pass


def _lay(pairs):
    o = _L()
    c = 0
    for name, n in pairs:
        setattr(o, name, c)
        c += n
    o.n = c
    return o


FBL = _lay([("i128", 128), ("tri", 128), ("ones", 128), ("minfS", 128),
            ("minfI", 128), ("x0", 8), ("bq", 1), ("bk", 1), ("bv", 1),
            ("bg", 1), ("bo", 8), ("bab", 2), ("cm", 8), ("wabo", 32),
            ("rw", 64), ("oc1024", 1), ("oc1", 1), ("zero", 1), ("eps", 1), ("cmi", 8)])
BBL = _lay([("i128", 128), ("oc1", 1), ("oc1024", 1)])
RBL = _lay([("or1", 128)])
RFL = _lay([("or1", 128), ("eseg", 1024)])

RL_MU, RL_RH, RL_RL, RL_E1H, RL_E1L, RL_E2H, RL_E2L, RL_G = range(8)
RA_MU, RA_RH, RA_RL, RA_M2R = range(4)

(CL_LA, CL_LAL, CL_LNR, CL_BIASE, CL_X0R, CL_KTIL, CL_GL, CL_BETA,
 CL_LNB, CL_R, CL_RH, CL_RL, CL_T1, CL_T2, CL_T3, CL_T4) = [16 * i for i in range(16)]


def build_program(dbg=()):
    dbg = set(dbg)
    nc = bass.Bass("TRN2", target_bir_lowering=False)

    d_x = nc.dram_tensor("xT", [D, TC], F32, kind="ExternalInput").ap()
    d_wq = nc.dram_tensor("wqT", [D, 128], F32, kind="ExternalInput").ap()
    d_wk = nc.dram_tensor("wkT", [D, 128], F32, kind="ExternalInput").ap()
    d_wv = nc.dram_tensor("wvT", [D, 128], F32, kind="ExternalInput").ap()
    d_wg = nc.dram_tensor("wgT", [D, 128], F32, kind="ExternalInput").ap()
    d_wo = nc.dram_tensor("woT", [128, D], F32, kind="ExternalInput").ap()
    d_ld = nc.dram_tensor("ldT", [D, 128], F32, kind="ExternalInput").ap()
    d_lu = nc.dram_tensor("luT", [128, D], F32, kind="ExternalInput").ap()
    d_fb = nc.dram_tensor("fblob", [128, FBL.n], F32, kind="ExternalInput").ap()
    d_bb = nc.dram_tensor("bblob", [128, BBL.n], BF16, kind="ExternalInput").ap()
    d_rb = nc.dram_tensor("rblob", [1, RBL.n], BF16, kind="ExternalInput").ap()
    d_rf = nc.dram_tensor("rfblob", [1, RFL.n], F32, kind="ExternalInput").ap()
    d_y = nc.dram_tensor("y", [D, TC], F32, kind="ExternalOutput").ap()
    d_pqloc = nc.dram_tensor("pqloc", [128, 512], BF16, kind="Internal").ap()
    d_pqall = nc.dram_tensor("pqall", [NCORES * 128, 512], BF16, kind="Internal",
                             addr_space="Shared").ap()
    d_g2loc = nc.dram_tensor("g2loc", [128, 512], F32, kind="Internal").ap()
    d_bari = nc.dram_tensor("bari", [128, 512], F32, kind="Internal").ap()
    d_baro = nc.dram_tensor("baro", [128, 512], F32, kind="Internal").ap()
    d_g2all = nc.dram_tensor("g2all", [NCORES * 128, 512], F32, kind="Internal",
                             addr_space="Shared").ap()
    d_rows4 = nc.dram_tensor("scr_rows4", [4, TC], F32, kind="Internal").ap()
    d_rows1 = nc.dram_tensor("scr_rows1", [1, TC], F32, kind="Internal").ap()
    d_combo = nc.dram_tensor("scr_combo", [128, 128], F32, kind="Internal").ap()
    d_wsr = nc.dram_tensor("scr_wsr", [8, 128], F32, kind="Internal").ap()

    d_dbg = {}
    for name, shape, dt in (("x1", [D, TC], F32), ("x2", [D, TC], F32),
                            ("qf", [128, TC], BF16), ("kf", [128, TC], BF16),
                            ("vf", [128, TC], BF16), ("gf", [128, TC], BF16),
                            ("og", [128, TC], BF16), ("rl", [1, 8 * TC], BF16),
                            ("sX7", [128, NCH * 384], BF16), ("sE2", [128, NCH * 128], BF16),
                            ("sKn", [128, NCH * 128], BF16), ("sWf", [128, NCH * 128], BF16),
                            ("cl", [128, 256], F32), ("Rf", [128, 256], F32),
                            ("pqa", [NCORES * 128, 256], F32),
                            ("Pc", [128, 256], F32), ("Qc", [128, 256], F32),
                            ("cps", [128, 512], BF16)):
        if name in dbg:
            d_dbg[name] = nc.dram_tensor("dbg_" + name, shape, dt,
                                         kind="ExternalOutput").ap()

    ser = Ser()
    ctx = ExitStack()
    sems = {e: ctx.enter_context(nc.semaphore("S_" + e)) for e in ENGINES}
    for e in ENGINES:
        sems[e + "_d"] = ctx.enter_context(nc.semaphore("SD_" + e))
    sems["gpsimd_c"] = ctx.enter_context(nc.semaphore("SC_gpsimd"))
    sems["scan"] = ctx.enter_context(nc.semaphore("S_scan"))
    sems["scan_c"] = ctx.enter_context(nc.semaphore("S_scan_c"))
    sb_names, ps_names = set(), set()

    def sb(name, shape, dtype):
        sb_names.add(name)
        return ctx.enter_context(nc.sbuf_tensor(name, shape, dtype))

    def psum(name, shape, dtype):
        ps_names.add(name)
        return ctx.enter_context(nc.psum_tensor(name, shape, dtype))

    xm = sb("xm", [128, FC * TC], F32)
    qf = sb("qf", [128, TC], BF16)
    kf = sb("kf", [128, TC], BF16)
    vf = sb("vf", [128, TC], BF16)
    gf = sb("gf", [128, TC], BF16)
    dgog = sb("dgog", [128, TC], BF16)
    xbsc = sb("xbsc", [128, 2 * TW], BF16)
    sqs = sb("sqs", [128, 2 * TW], BF16)
    wqb = sb("wqb", [128, D], BF16)
    wkb = sb("wkb", [128, D], BF16)
    wvb = sb("wvb", [128, D], BF16)
    wgb = sb("wgb", [128, D], BF16)
    wob = sb("wob", [128, D], BF16)
    ldb = sb("ldb", [128, D], BF16)
    lub = sb("lub", [128, D], BF16)
    fb = sb("fb", [128, FBL.n], F32)
    bbs = sb("bbs", [128, BBL.n], BF16)
    rbs = sb("rbs", [1, RBL.n], BF16)
    rfs = sb("rfs", [1, RFL.n], F32)
    wabob = sb("wabob", [128, 32], BF16)
    sKn = sb("sKn", [128, NCH * 128], BF16)
    sWf = sb("sWf", [128, NCH * 128], BF16)
    sE2 = sb("sE2", [128, NCH * 128], BF16)
    sX7 = sb("sX7", [128, NCH * 384], BF16)
    Xa = sb("Xa", [128, 384], BF16)
    Xb = sb("Xb", [128, 384], BF16)
    cps = sb("cps", [128, 4 * 128], BF16)
    scrA = sb("scrA", [128, 128], BF16)
    scrB = sb("scrB", [128, 128], BF16)
    scrD = sb("scrD", [128, 128], BF16)
    sRj = sb("sRj", [128, NCH * 128], BF16)
    crs = sb("crs", [128, 6 * 128], BF16)
    Pc = sb("Pc", [128, 2 * 128], F32)
    Qc = sb("Qc", [128, 2 * 128], F32)
    Pcb = sb("Pcb", [128, 2 * 128], BF16)
    Qcb = sb("Qcb", [128, 2 * 128], BF16)
    Rf = sb("Rf", [128, 2 * 128], F32)
    PQin = sb("PQin", [128, 256], BF16)
    rowsA = sb("rowsA", [4, TC], F32)
    rowsB = sb("rowsB", [1, TC], F32)
    colsAB = sb("colsAB", [128, 64], F32)
    colsEx = None
    colsN2 = None
    cl = sb("cl", [128, 256], F32)
    combo = sb("combo", [128, 128], F32)
    rland = sb("rland", [1, 8 * TC], BF16)
    wsum8 = sb("wsum8", [128, 8], F32)
    wsrows = sb("wsrows", [1, 1024], BF16)
    rbcS = sb("rbcS", [128, 2 * TW], F32)
    xr = sb("xr", [128, TW], BF16)
    t0f = sb("t0f", [128, 64], F32)
    t0r = sb("t0r", [1, 64], F32)
    gcol = sb("gcol", [128, 4], F32)

    pA = psum("pA", [128, TW], F32)
    pB = psum("pB", [128, TW], F32)
    pC = psum("pC", [128, TW], F32)
    pS1 = psum("pS1", [128, TW], F32)
    pS2 = psum("pS2", [128, TW], F32)
    pS3 = psum("pS3", [128, TW], F32)
    pD = psum("pD", [128, TW], F32)
    pT = psum("pT", [128, 2 * TW], BF16)

    T0_SQ, T0_LN, T0_X2 = 0, 8, 24
    T0_MU, T0_R, T0_LN3 = 46, 47, 48

    def V(fn):
        ser.op("vector", fn)

    def Sc(fn):
        ser.op("scalar", fn)

    def Tn(fn):
        ser.op("tensor", fn)

    def G(fn):
        ser.op("gpsimd", fn)

    def fcol(off, n=1):
        return fb[0:128, off:off + n]

    ident = fb[0:128, FBL.i128:FBL.i128 + 128]
    or1b = rbs[0:1, RBL.or1:RBL.or1 + 128]
    or1f = rfs[0:1, RFL.or1:RFL.or1 + 128]
    ib128 = bbs[0:128, BBL.i128:BBL.i128 + 128]
    oc1b = bbs[0:128, BBL.oc1:BBL.oc1 + 1]
    oc1024b = bbs[0:128, BBL.oc1024:BBL.oc1024 + 1]

    def clc(off, n=16):
        return cl[0:128, off:off + n]

    def clj(off, j):
        return cl[0:128, off + j:off + j + 1]

    def xms(ci, t):
        return xm[:, ci * TC + t * TW: ci * TC + (t + 1) * TW]

    def rls(q, t):
        return rland[0:1, q * TC + t * TW: q * TC + (t + 1) * TW]

    def rlj(q, j):
        return rland[0:1, q * TC + j * 128: q * TC + (j + 1) * 128]

    ser.dma("sync", lambda e: e.dma_start(fb[:, :], d_fb))
    ser.dma("sync", lambda e: e.dma_start(bbs[:, :], d_bb))
    ser.dma("sync", lambda e: e.dma_start(rbs[:, :], d_rb))
    ser.dma("sync", lambda e: e.dma_start(rfs[:, :], d_rf))
    for ci in range(FC):
        ser.dma("sync" if ci % 2 == 0 else "scalar", lambda e, ci=ci: e.dma_start(
            xm[:, ci * TC:(ci + 1) * TC], d_x[ci * 128:(ci + 1) * 128, 0:TC]))
    for dst, src, rearr in ((wqb, d_wq, True), (wkb, d_wk, True), (wvb, d_wv, True),
                            (wgb, d_wg, True), (ldb, d_ld, True),
                            (wob, d_wo, False), (lub, d_lu, False)):
        if rearr:
            ser.dma("gpsimd", lambda e, dst=dst, src=src: e.dma_start(
                dst[:, :].rearrange("p (ci c) -> p ci c", ci=FC),
                src.rearrange("(ci p) c -> p ci c", p=128)))
        else:
            ser.dma("gpsimd", lambda e, dst=dst, src=src: e.dma_start(dst[:, :], src))
    V(lambda e: e.tensor_copy(wabob[:, :], fb[:, FBL.wabo:FBL.wabo + 32]))
    V(lambda e: e.memset(Qc[:, 0:128], 0.0))
    V(lambda e: e.memset(rbcS[:, 0:TW], 0.0))
    ser.dma("gpsimd", lambda e: e.dma_start(d_pqloc[0:128, 0:512], rbcS[:, 0:TW]))
    ser.dma("sync", lambda e: e.dma_start(d_g2loc[0:128, 0:512], rbcS[:, 0:TW]))
    ser.dma("sync", lambda e: e.dma_start(d_bari[0:128, 0:512], rbcS[:, 0:TW]))

    for i, w in enumerate((wqb, wkb, wvb, wgb, ldb)):
        def f(e, w=w):
            for ci in range(FC):
                ins = e.matmul(pS1[0:128, 0:1], w[:, ci * 128:(ci + 1) * 128],
                               oc1b, start=(ci == 0), stop=(ci == 7))
            return ins
        Tn(f)
        V(lambda e, i=i: e.tensor_copy(wsum8[:, i:i + 1], pS1[0:128, 0:1]))
    Tn(lambda e: e.transpose(pS1[0:5, 0:128], wsum8[:, 0:5], ident))
    Sc(lambda e: e.mul(combo[0:5, 0:128], pS1[0:5, 0:128], -1.0))
    ser.dma("sync", lambda e: e.dma_start(d_wsr[0:5, 0:128], combo[0:5, 0:128]))
    ser.dma("gpsimd", lambda e: e.dma_start(
        wsrows[0:1, 0:640], d_wsr[0:5, 0:128].rearrange("p t -> (p t)")))

    def f_wsab(e):
        for ci in range(FC):
            ins = e.matmul(pS1[0:4, 0:1], wabob[:, 4 * ci:4 * ci + 4],
                           oc1b, start=(ci == 0), stop=(ci == 7))
        return ins
    Tn(f_wsab)
    V(lambda e: e.tensor_copy(t0f[0:4, T0_SQ:T0_SQ + 1], pS1[0:4, 0:1]))
    Tn(lambda e: e.transpose(pS2[0:1, 0:4], t0f[0:4, T0_SQ:T0_SQ + 1],
                             fb[0:4, FBL.i128:FBL.i128 + 4]))
    V(lambda e: e.tensor_copy(t0r[0:1, 17:21], pS2[0:1, 0:4]))
    Tn(lambda e: e.matmul(pS1[0:128, 0:1], or1f, t0r[0:1, 17:18], start=True, stop=True))
    V(lambda e: e.tensor_copy(clc(CL_T3, 1), pS1[0:128, 0:1]))
    Tn(lambda e: e.matmul(pS1[0:128, 0:1], or1f, t0r[0:1, 18:19], start=True, stop=True))
    V(lambda e: e.tensor_copy(clc(CL_T4, 1), pS1[0:128, 0:1]))

    def t0_lnstats(src, dst):
        Tn(lambda e: e.matmul(pS1[0:1, 0:8], fcol(FBL.oc1), src, start=True, stop=True))
        V(lambda e: e.reduce_sum(t0r[0:1, 12:13], pS1[0:1, 0:8], axis=AX.X))
        Sc(lambda e: e.activation(t0r[0:1, 8:9], t0r[0:1, 12:13], AF.Copy, scale=1.0 / 1024.0))
        Sc(lambda e: e.activation(t0f[:, T0_SQ:T0_SQ + 8], src, AF.Square))
        Tn(lambda e: e.matmul(pS1[0:1, 0:8], fcol(FBL.oc1), t0f[:, T0_SQ:T0_SQ + 8],
                              start=True, stop=True))
        V(lambda e: e.reduce_sum(t0r[0:1, 12:13], pS1[0:1, 0:8], axis=AX.X))
        V(lambda e: e.tensor_tensor(t0r[0:1, 9:10], t0r[0:1, 8:9], t0r[0:1, 8:9], OP.mult))
        V(lambda e: e.tensor_scalar(t0r[0:1, 10:11], t0r[0:1, 12:13], 1.0 / 1024.0, None, OP.mult))
        V(lambda e: e.tensor_tensor(t0r[0:1, 10:11], t0r[0:1, 10:11], t0r[0:1, 9:10], OP.subtract))
        Sc(lambda e: e.activation(t0r[0:1, 11:12], t0r[0:1, 10:11], AF.Ln, bias=fb[0:1, FBL.eps:FBL.eps + 1]))
        Sc(lambda e: e.activation(t0r[0:1, 11:12], t0r[0:1, 11:12], AF.Exp, scale=-0.5))
        Tn(lambda e: e.matmul(pS1[0:128, 0:1], or1f, t0r[0:1, 8:9], start=True, stop=True))
        V(lambda e: e.tensor_copy(t0f[:, T0_MU:T0_MU + 1], pS1[0:128, 0:1]))
        Tn(lambda e: e.matmul(pS1[0:128, 0:1], or1f, t0r[0:1, 11:12], start=True, stop=True))
        V(lambda e: e.tensor_copy(t0f[:, T0_R:T0_R + 1], pS1[0:128, 0:1]))
        V(lambda e: e.tensor_scalar(t0f[:, dst:dst + 8], src,
                                    t0f[:, T0_MU:T0_MU + 1], None, OP.subtract))
        V(lambda e: e.tensor_scalar(t0f[:, dst:dst + 8], t0f[:, dst:dst + 8],
                                    t0f[:, T0_R:T0_R + 1], None, OP.mult))

    def t0_gates(ln_col, gi):
        def f(e):
            for ci in range(FC):
                ins = e.matmul(pS1[0:1, 0:8], t0f[:, ln_col + ci:ln_col + ci + 1],
                               fb[:, FBL.rw + ci * 8:FBL.rw + ci * 8 + 8],
                               start=(ci == 0), stop=(ci == 7))
            return ins
        Tn(f)
        V(lambda e: e.tensor_copy(t0r[0:1, 0:8], pS1[0:1, 0:8]))
        V(lambda e: e.reduce_max(t0r[0:1, 8:9], t0r[0:1, 0:8], axis=AX.X))
        V(lambda e: e.tensor_scalar(t0r[0:1, 0:8], t0r[0:1, 0:8], t0r[0:1, 8:9],
                                    None, OP.subtract))
        V(lambda e: e.tensor_scalar(t0r[0:1, 21:29], t0r[0:1, 0:8], 0.0, None, OP.is_ge))
        V(lambda e: e.scalar_tensor_tensor(t0r[0:1, 21:29], t0r[0:1, 21:29], NEG,
                                           t0r[0:1, 0:8], OP.mult, OP.add))
        V(lambda e: e.reduce_max(t0r[0:1, 9:10], t0r[0:1, 21:29], axis=AX.X))
        V(lambda e: e.tensor_scalar(t0r[0:1, 29:37], t0r[0:1, 0:8], t0r[0:1, 9:10],
                                    None, OP.is_ge))
        Sc(lambda e: e.activation(t0r[0:1, 37:45], t0r[0:1, 0:8], AF.Exp))
        V(lambda e: e.tensor_tensor(t0r[0:1, 37:45], t0r[0:1, 37:45], t0r[0:1, 29:37], OP.mult))
        V(lambda e: e.reduce_sum(t0r[0:1, 10:11], t0r[0:1, 37:45], axis=AX.X))
        V(lambda e: e.reciprocal(t0r[0:1, 11:12], t0r[0:1, 10:11]))
        V(lambda e: e.tensor_scalar(t0r[0:1, 37:45], t0r[0:1, 37:45], t0r[0:1, 11:12],
                                    None, OP.mult))
        def f2(e):
            for ee in range(8):
                ins = e.matmul(pS1[0:128, 0:1],
                               rfs[0:1, RFL.eseg + ee * 128:RFL.eseg + (ee + 1) * 128],
                               t0r[0:1, 37 + ee:38 + ee], start=(ee == 0), stop=(ee == 7))
            return ins
        Tn(f2)
        V(lambda e, gi=gi: e.tensor_copy(gcol[:, gi:gi + 1], pS1[0:128, 0:1]))
        V(lambda e, gi=gi: e.tensor_scalar(gcol[:, gi + 1:gi + 2], gcol[:, gi:gi + 1],
                                           0.5, None, OP.mult))

    t0_lnstats(fb[:, FBL.x0:FBL.x0 + 8], T0_LN)
    t0_gates(T0_LN, 0)

    def emit_row_to_cols(rows_ap, dram, nq, dst):
        ser.dma("sync", lambda e: e.dma_start(dram[0:nq, 0:TC], rows_ap))
        ser.dma("sync", lambda e: e.dma_start(
            combo[0:nq * 16, 0:128],
            dram[0:nq, 0:TC].rearrange("q (j t) -> (q j) t", t=128)))
        Tn(lambda e: e.transpose(pS1[0:128, 0:nq * 16], combo[0:nq * 16, 0:128],
                                 fb[0:nq * 16, FBL.i128:FBL.i128 + nq * 16]))
        V(lambda e: e.tensor_copy(dst[0:128, 0:nq * 16], pS1[0:128, 0:nq * 16]))

    def emit_cols_to_rows(src_ap, nq, q0):
        Tn(lambda e: e.transpose(pS1[0:nq * 16, 0:128], src_ap, ident))
        V(lambda e: e.tensor_copy(combo[0:nq * 16, 0:128], pS1[0:nq * 16, 0:128]))
        ser.dma("sync", lambda e: e.dma_start(d_combo[0:nq * 16, 0:128],
                                              combo[0:nq * 16, 0:128]))
        ser.dma("gpsimd", lambda e: e.dma_start(
            rland[0:1, q0 * TC:(q0 + nq) * TC],
            d_combo[0:nq * 16, 0:128].rearrange("p t -> (p t)")))

    def emit_r_from_muex(mu_ap, ex_ap):
        V(lambda e: e.tensor_tensor(clc(CL_T1), mu_ap, mu_ap, OP.mult))
        V(lambda e: e.tensor_tensor(clc(CL_T1), ex_ap, clc(CL_T1), OP.subtract))
        Sc(lambda e: e.activation(clc(CL_R), clc(CL_T1), AF.Ln, bias=fcol(FBL.eps)))
        Sc(lambda e: e.activation(clc(CL_R), clc(CL_R), AF.Exp, scale=-0.5))
        V(lambda e: e.tensor_copy(scrA[:, 0:16], clc(CL_R)))
        V(lambda e: e.tensor_copy(clc(CL_RH), scrA[:, 0:16]))
        V(lambda e: e.tensor_tensor(clc(CL_RL), clc(CL_R), clc(CL_RH), OP.subtract))

    def rbs_t(t):
        return rbcS[:, (t % 2) * TW:(t % 2 + 1) * TW]

    def emit_rbc(t):
        def f(e, t=t):
            e.matmul(pB[0:128, 0:TW], or1b, rls(RL_RH, t), start=True, stop=False)
            return e.matmul(pB[0:128, 0:TW], or1b, rls(RL_RL, t), start=False, stop=True)
        Tn(f)
        Sc(lambda e, t=t: e.copy(rbs_t(t), pB[0:128, 0:TW]))

    def stats_AD_inline():
        for t in range(NT):
            for ci in range(FC):
                xs = xbsc[:, (ci % 2) * TW:(ci % 2 + 1) * TW]
                sq_ = sqs[:, (ci % 2) * TW:(ci % 2 + 1) * TW]
                if ci % 2 == 0:
                    V(lambda e, t=t, ci=ci, xs=xs: e.tensor_copy(xs, xms(ci, t)))
                else:
                    Sc(lambda e, t=t, ci=ci, xs=xs: e.copy(xs, xms(ci, t)))
                Tn(lambda e, ci=ci, xs=xs: e.matmul(pC[0:1, 0:TW], oc1024b, xs,
                                                    start=(ci == 0), stop=(ci == 7)))
                V(lambda e, xs=xs, sq_=sq_: e.tensor_tensor(sq_, xs, xs, OP.mult))
                Tn(lambda e, ci=ci, sq_=sq_: e.matmul(pB[0:1, 0:TW], oc1024b, sq_,
                                                      start=(ci == 0), stop=(ci == 7)))
            Sc(lambda e, t=t: e.copy(rowsB[0:1, t * TW:(t + 1) * TW], pC[0:1, 0:TW]))
            V(lambda e, t=t: e.tensor_copy(rowsA[0:1, t * TW:(t + 1) * TW], pB[0:1, 0:TW]))
        emit_row_to_cols(rowsB[0:1, 0:TC], d_rows1, 1, colsAB[0:128, 48:64])
        V(lambda e: e.tensor_copy(clc(CL_T2), colsAB[:, 48:64]))
        emit_row_to_cols(rowsA[0:1, 0:TC], d_rows1, 1, colsAB[0:128, 48:64])
        emit_r_from_muex(clc(CL_T2), colsAB[:, 48:64])
        V(lambda e: e.scalar_tensor_tensor(clc(CL_T3), clc(CL_T2), -0.5, clc(CL_R),
                                           OP.mult, OP.mult))
        V(lambda e: e.tensor_copy(combo[:, 0:16], clc(CL_T2)))
        V(lambda e: e.tensor_copy(combo[:, 16:32], clc(CL_RH)))
        V(lambda e: e.tensor_copy(combo[:, 32:48], clc(CL_RL)))
        V(lambda e: e.tensor_copy(combo[:, 48:64], clc(CL_T3)))
        emit_cols_to_rows(combo[:, 0:64], 4, 0)

    def emit_ffn(ghalf_col):
        def emit_down(t):
            pDn = pA if t % 2 == 0 else pC
            emit_rbc(t)
            def fdown(e, t=t, pDn=pDn):
                return e.matmul(pDn[0:128, 0:TW], wsrows[0:1, 512:640],
                                rls(RA_MU, t), start=False, stop=True)
            for ci in range(FC):
                xs = xbsc[:, (ci % 2) * TW:(ci % 2 + 1) * TW]
                if ci % 2 == 0:
                    V(lambda e, t=t, ci=ci, xs=xs: e.tensor_copy(xs, xms(ci, t)))
                else:
                    Sc(lambda e, t=t, ci=ci, xs=xs: e.copy(xs, xms(ci, t)))
                Tn(lambda e, ci=ci, xs=xs, pDn=pDn: e.matmul(
                    pDn[0:128, 0:TW], ldb[:, ci * 128:(ci + 1) * 128],
                    xs, start=(ci == 0), stop=False))
            Tn(fdown)

        emit_down(0)
        for t in range(NT):
            pDn = pA if t % 2 == 0 else pC
            if t + 1 < NT:
                emit_down(t + 1)
            V(lambda e, t=t, pDn=pDn: e.scalar_tensor_tensor(
                dgog[:, t * TW:(t + 1) * TW], pDn[0:128, 0:TW],
                gcol[:, ghalf_col:ghalf_col + 1], rbs_t(t), OP.mult, OP.mult))
            def xrp_of(ci):
                return xr[:, :] if ci % 2 == 0 else sqs[:, 0:TW]

            def emit_xr(ci, t=t):
                V(lambda e, t=t, ci=ci: e.scalar_tensor_tensor(
                    xrp_of(ci), xms(ci, t), 0.5, rbs_t(t), OP.mult, OP.mult))
            emit_xr(0)
            emit_xr(1)
            for ci in range(FC):
                pU = (pS2, pS3, pS1, pD)[ci % 4]
                def fup(e, t=t, ci=ci, pU=pU):
                    e.matmul(pU[0:128, 0:TW], lub[:, ci * 128:(ci + 1) * 128],
                             dgog[:, t * TW:(t + 1) * TW], start=True, stop=False)
                    e.matmul(pU[0:128, 0:TW], or1b, rls(RA_M2R, t),
                             start=False, stop=False)
                    return e.matmul(pU[0:128, 0:TW], ib128, xrp_of(ci),
                                    start=False, stop=True)
                Tn(fup)
                if ci + 2 < FC:
                    emit_xr(ci + 2)
                V(lambda e, t=t, ci=ci, pU=pU: e.scalar_tensor_tensor(
                    xms(ci, t), pU[0:128, 0:TW], 0.0, xms(ci, t), OP.bypass, OP.add))

    stats_AD_inline()
    emit_ffn(1)

    if "x1" in d_dbg:
        for ci in range(FC):
            ser.dma("sync", lambda e, ci=ci: e.dma_start(
                d_dbg["x1"][ci * 128:(ci + 1) * 128, 0:TC], xm[:, ci * TC:(ci + 1) * TC]))

    for t in range(NT):
        for ci in range(FC):
            xs = xbsc[:, (ci % 2) * TW:(ci % 2 + 1) * TW]
            sq_ = sqs[:, (ci % 2) * TW:(ci % 2 + 1) * TW]
            if ci % 2 == 0:
                V(lambda e, t=t, ci=ci, xs=xs: e.tensor_copy(xs, xms(ci, t)))
            else:
                Sc(lambda e, t=t, ci=ci, xs=xs: e.copy(xs, xms(ci, t)))
            Tn(lambda e, ci=ci, xs=xs: e.matmul(pC[0:4, 0:TW], wabob[:, 4 * ci:4 * ci + 4],
                                                xs, start=(ci == 0), stop=(ci == 7)))
            V(lambda e, xs=xs, sq_=sq_: e.tensor_tensor(sq_, xs, xs, OP.mult))
            Tn(lambda e, ci=ci, sq_=sq_: e.matmul(pB[0:1, 0:TW], oc1024b, sq_,
                                                  start=(ci == 0), stop=(ci == 7)))
        Sc(lambda e, t=t: e.copy(rowsA[0:4, t * TW:(t + 1) * TW], pC[0:4, 0:TW]))
        V(lambda e, t=t: e.tensor_copy(rowsB[0:1, t * TW:(t + 1) * TW], pB[0:1, 0:TW]))
    emit_row_to_cols(rowsA[0:4, 0:TC], d_rows4, 4, colsAB)
    emit_row_to_cols(rowsB[0:1, 0:TC], d_rows1, 1, colsAB[0:128, 48:64])
    zaC = colsAB[:, 0:16]
    zbC = colsAB[:, 16:32]
    muC = colsAB[:, 32:48]
    emit_r_from_muex(muC, colsAB[:, 48:64])
    V(lambda e: e.tensor_copy(combo[:, 0:16], muC))
    V(lambda e: e.tensor_copy(combo[:, 16:32], clc(CL_RH)))
    V(lambda e: e.tensor_copy(combo[:, 32:48], clc(CL_RL)))
    emit_cols_to_rows(combo[:, 0:48], 3, RL_MU)

    for t in range(NT):
        emit_rbc(t)
        for ci in range(FC):
            xs = xbsc[:, (ci % 2) * TW:(ci % 2 + 1) * TW]
            if ci % 2 == 0:
                V(lambda e, t=t, ci=ci, xs=xs: e.tensor_copy(xs, xms(ci, t)))
            else:
                Sc(lambda e, t=t, ci=ci, xs=xs: e.copy(xs, xms(ci, t)))
            for pp, w in ((pA, wqb), (pS2, wkb), (pS3, wvb), (pC, wgb)):
                Tn(lambda e, pp=pp, w=w, ci=ci, xs=xs: e.matmul(
                    pp[0:128, 0:TW], w[:, ci * 128:(ci + 1) * 128], xs,
                    start=(ci == 0), stop=False))
        for pp, wsoff in ((pA, 0), (pS2, 128), (pS3, 256), (pC, 384)):
            Tn(lambda e, pp=pp, wsoff=wsoff, t=t: e.matmul(
                pp[0:128, 0:TW], wsrows[0:1, wsoff:wsoff + 128], rls(RL_MU, t),
                start=False, stop=True))
        for pi, (pp, dst, boff, sig) in enumerate(
                ((pA, qf, FBL.bq, False), (pS2, kf, FBL.bk, False),
                 (pS3, vf, FBL.bv, False), (pC, gf, FBL.bg, True))):
            scx = sqs[:, 0:TW] if pi % 2 == 0 else xr[:, :]
            V(lambda e, pp=pp, scx=scx, t=t: e.tensor_tensor(scx, pp[0:128, 0:TW],
                                                             rbs_t(t), OP.mult))
            if sig:
                Sc(lambda e, dst=dst, t=t, boff=boff, scx=scx: e.activation(
                    dst[:, t * TW:(t + 1) * TW], scx, AF.Sigmoid,
                    bias=fcol(boff)))
            else:
                V(lambda e, dst=dst, t=t, boff=boff, scx=scx: e.tensor_scalar(
                    dst[:, t * TW:(t + 1) * TW], scx, fcol(boff), None, OP.add))

    for name in ("qf", "kf", "vf", "gf"):
        if name in d_dbg:
            src = {"qf": qf, "kf": kf, "vf": vf, "gf": gf}[name]
            ser.dma("sync", lambda e, src=src, name=name: e.dma_start(
                d_dbg[name][0:128, 0:TC], src[:, 0:TC]))

    for t in range(NT):
        sq_ = sqs[:, (t % 2) * TW:(t % 2 + 1) * TW]
        V(lambda e, t=t, sq_=sq_: e.tensor_tensor(sq_, kf[:, t * TW:(t + 1) * TW],
                                                  kf[:, t * TW:(t + 1) * TW], OP.mult))
        Tn(lambda e, sq_=sq_, t=t: e.matmul(pB[0:1, 0:TW] if t % 2 == 0 else pC[0:1, 0:TW],
                                            oc1b, sq_, start=True, stop=True))
        Sc(lambda e, t=t: e.copy(rowsB[0:1, t * TW:(t + 1) * TW],
                                 pB[0:1, 0:TW] if t % 2 == 0 else pC[0:1, 0:TW]))
    emit_row_to_cols(rowsB[0:1, 0:TC], d_rows1, 1, colsAB[0:128, 48:64])

    def fold_z(zC, ws_off, bab_i, dst_off):
        V(lambda e: e.tensor_scalar(clc(CL_T1), muC, cl[0:128, ws_off:ws_off + 1],
                                    None, OP.mult))
        V(lambda e: e.tensor_tensor(clc(CL_T1), zC, clc(CL_T1), OP.subtract))
        V(lambda e: e.tensor_tensor(clc(CL_T1), clc(CL_T1), clc(CL_R), OP.mult))
        V(lambda e, dst_off=dst_off: e.tensor_scalar(
            cl[0:128, dst_off:dst_off + 16], clc(CL_T1),
            fb[0:128, FBL.bab + bab_i:FBL.bab + bab_i + 1], None, OP.add))

    fold_z(zbC, CL_T4, 1, CL_T2)
    Sc(lambda e: e.activation(clc(CL_BETA), clc(CL_T2), AF.Sigmoid))
    fold_z(zaC, CL_T3, 0, CL_T2)
    Sc(lambda e: e.activation(clc(CL_LNB), clc(CL_BETA), AF.Ln))
    Sc(lambda e: e.activation(clc(CL_T3), clc(CL_T2), AF.Relu, scale=-1.0))
    Sc(lambda e: e.activation(clc(CL_T4), clc(CL_T2), AF.Abs))
    Sc(lambda e: e.activation(clc(CL_T4), clc(CL_T4), AF.Exp, scale=-1.0))
    Sc(lambda e: e.activation(clc(CL_T4), clc(CL_T4), AF.Ln, bias=fcol(FBL.oc1)))
    V(lambda e: e.scalar_tensor_tensor(clc(CL_T2), clc(CL_T3), -1.0, clc(CL_T4),
                                       OP.mult, OP.subtract))
    Tn(lambda e: e.matmul(pS1[0:128, 0:16], fb[:, FBL.tri:FBL.tri + 128],
                          clc(CL_T2), start=True, stop=True))
    V(lambda e: e.tensor_copy(clc(CL_LA), pS1[0:128, 0:16]))
    Tn(lambda e: e.matmul(pS1[0:128, 0:16], fb[:, FBL.ones:FBL.ones + 128],
                          clc(CL_T2), start=True, stop=True))
    V(lambda e: e.tensor_copy(clc(CL_LAL), pS1[0:128, 0:16]))
    V(lambda e: e.tensor_scalar(clc(CL_T1), colsAB[:, 48:64], 1e-24, None, OP.max))
    Sc(lambda e: e.activation(clc(CL_LNR), clc(CL_T1), AF.Ln))
    V(lambda e: e.scalar_tensor_tensor(clc(CL_BIASE), clc(CL_LNR), -0.5, clc(CL_LA),
                                       OP.mult, OP.subtract))
    V(lambda e: e.tensor_tensor(clc(CL_BIASE), clc(CL_BIASE), clc(CL_LNB), OP.add))
    V(lambda e: e.scalar_tensor_tensor(clc(CL_T1), clc(CL_LNR), -0.5, clc(CL_LA),
                                       OP.mult, OP.add))
    Sc(lambda e: e.activation(clc(CL_X0R), clc(CL_T1), AF.Exp))
    V(lambda e: e.tensor_tensor(clc(CL_T2), clc(CL_LAL), clc(CL_BIASE), OP.add))
    Sc(lambda e: e.activation(clc(CL_KTIL), clc(CL_T2), AF.Exp))
    Sc(lambda e: e.activation(clc(CL_GL), clc(CL_LAL), AF.Exp))
    V(lambda e: e.tensor_copy(scrA[:, 0:16], clc(CL_T1)))
    V(lambda e: e.tensor_copy(clc(CL_T2), scrA[:, 0:16]))
    V(lambda e: e.tensor_copy(combo[:, 0:16], clc(CL_T2)))
    V(lambda e: e.tensor_tensor(combo[:, 16:32], clc(CL_T1), clc(CL_T2), OP.subtract))
    V(lambda e: e.tensor_copy(scrA[:, 0:16], clc(CL_LA)))
    V(lambda e: e.tensor_copy(clc(CL_T2), scrA[:, 0:16]))
    V(lambda e: e.tensor_copy(combo[:, 32:48], clc(CL_T2)))
    V(lambda e: e.tensor_tensor(combo[:, 48:64], clc(CL_LA), clc(CL_T2), OP.subtract))
    Sc(lambda e: e.activation(combo[:, 64:80], clc(CL_LA), AF.Exp))
    emit_cols_to_rows(combo[:, 0:80], 5, RL_E1H)
    if "rl" in d_dbg:
        ser.dma("sync", lambda e: e.dma_start(d_dbg["rl"][0:1, 0:8 * TC],
                                              rland[0:1, 0:8 * TC]))

    def kslice(s, j):
        return s[:, j * 128:(j + 1) * 128]

    def x7(j, part):
        return sX7[:, j * 384 + part * 128: j * 384 + (part + 1) * 128]

    dgA = dgog[:, 0:384]
    dgB = dgog[:, 384:768]
    dgCPS = dgog[:, 768:1280]
    dgE1 = dgog[:, 1280:1408]
    dgMn = dgog[:, 1408:1536]
    dgKt = dgog[:, 1536:1664]

    def eager_steps(j):
        p = j % 2
        XA = Xa if p == 0 else dgA
        XB = Xb if p == 0 else dgB
        CPS = cps if p == 0 else dgCPS
        E1 = scrA[:, :] if p == 0 else dgE1
        MN = scrB[:, :] if p == 0 else dgMn
        KT = scrD[:, :] if p == 0 else dgKt
        AE = combo[0:128, 0:128] if p == 0 else rbcS[0:128, 0:128]
        pX = pD if p == 0 else pA
        pV = pS1 if p == 0 else pB
        pG = pS2 if p == 0 else pC
        st = []
        def T(fn, glue=False): st.append(("tensor", fn, glue))
        def Vs(fn, glue=False): st.append(("vector", fn, glue))
        def Ss(fn, glue=False): st.append(("scalar", fn, glue))
        T(lambda e, j=j: e.transpose(pT[0:128, 0:128], kslice(kf, j), ib128), glue=True)
        Ss(lambda e, j=j: e.copy(kslice(sKn, j), pT[0:128, 0:128]))
        Vs(lambda e, j=j, XA=XA: e.tensor_scalar(XA[:, 128:256], kslice(sKn, j),
                                                 clj(CL_X0R, j), None, OP.mult))
        Vs(lambda e, j=j, KT=KT: e.tensor_scalar(KT, kslice(sKn, j),
                                                 clj(CL_KTIL, j), None, OP.mult))
        T(lambda e, j=j: e.transpose(pT[0:128, 0:128], kslice(vf, j), ib128), glue=True)
        Ss(lambda e, XA=XA: e.copy(XA[:, 0:128], pT[0:128, 0:128]))
        T(lambda e, j=j, pV=pV: e.matmul(pV[0:128, 0:128], kslice(vf, j), kslice(qf, j),
                                         start=True, stop=True))
        Vs(lambda e, XA=XA, pV=pV: e.tensor_copy(XA[:, 256:384], pV[0:128, 0:128]))
        T(lambda e, j=j, pG=pG: e.matmul(pG[0:128, 0:128], kslice(kf, j), kslice(kf, j),
                                         start=True, stop=True))
        def fbc1(e, j=j, pV=pV):
            e.matmul(pV[0:128, 0:128], or1b, rlj(RL_E1H, j), start=True, stop=False)
            return e.matmul(pV[0:128, 0:128], or1b, rlj(RL_E1L, j), start=False, stop=True)
        T(fbc1)
        Vs(lambda e, AE=AE, pV=pV: e.scalar_tensor_tensor(AE, pV[0:128, 0:128], 0.0,
                                                          fb[:, FBL.minfS:FBL.minfS + 128],
                                                          OP.bypass, OP.add))
        Ss(lambda e, j=j, E1=E1, AE=AE: e.activation(E1, AE, AF.Exp,
                                                     bias=clj(CL_BIASE, j)))
        Vs(lambda e, E1=E1, pG=pG, CPS=CPS: e.scalar_tensor_tensor(
            CPS[:, 0:128], pG[0:128, 0:128], -1.0, E1, OP.mult, OP.mult))
        T(lambda e, CPS=CPS: e.transpose(pT[0:128, 0:128], CPS[:, 0:128], ib128), glue=True)
        Ss(lambda e, CPS=CPS: e.copy(CPS[:, 128:256], pT[0:128, 0:128]))
        def fbc2(e, j=j, pV=pV):
            e.matmul(pV[0:128, 0:128], or1b, rlj(RL_E2H, j), start=True, stop=False)
            return e.matmul(pV[0:128, 0:128], or1b, rlj(RL_E2L, j), start=False, stop=True)
        T(fbc2)
        Vs(lambda e, AE=AE, pV=pV: e.scalar_tensor_tensor(AE, pV[0:128, 0:128], 0.0,
                                                          fb[:, FBL.minfI:FBL.minfI + 128],
                                                          OP.bypass, OP.add))
        Ss(lambda e, j=j, AE=AE: e.activation(kslice(sE2, j), AE, AF.Exp,
                                              bias=clj(CL_BIASE, j)))
        for k in range(7):
            cur = (k % 2) * 256
            nxt = 256 - cur
            Xcur = XA if k % 2 == 0 else XB
            Xnxt = XB if k % 2 == 0 else XA
            last = (k == 6)
            xdst = sX7[:, j * 384:(j + 1) * 384] if last else Xnxt[:, 0:384]
            T(lambda e, cur=cur, Xcur=Xcur, CPS=CPS, pX=pX: e.matmul(
                pX[0:128, 0:384], CPS[:, cur:cur + 128], Xcur[:, 0:384],
                start=True, stop=True))
            Vs(lambda e, Xcur=Xcur, xdst=xdst, pX=pX: e.scalar_tensor_tensor(
                xdst, pX[0:128, 0:384], 0.0, Xcur[:, 0:384], OP.bypass, OP.add))
            if not last:
                T(lambda e, cur=cur, CPS=CPS, pV=pV: e.matmul(
                    pV[0:128, 0:128], CPS[:, cur:cur + 128],
                    CPS[:, cur + 128:cur + 256], start=True, stop=True))
                Ss(lambda e, nxt=nxt, CPS=CPS, pV=pV: e.copy(
                    CPS[:, nxt + 128:nxt + 256], pV[0:128, 0:128]))
                T(lambda e, cur=cur, CPS=CPS, pG=pG: e.matmul(
                    pG[0:128, 0:128], CPS[:, cur + 128:cur + 256],
                    CPS[:, cur:cur + 128], start=True, stop=True))
                Vs(lambda e, nxt=nxt, CPS=CPS, pG=pG: e.tensor_copy(
                    CPS[:, nxt:nxt + 128], pG[0:128, 0:128]))
        T(lambda e, j=j: e.transpose(pT[0:128, 0:128], x7(j, 1), ib128), glue=True)
        Ss(lambda e, j=j: e.copy(kslice(sWf, j), pT[0:128, 0:128]))
        cur = (j % 2) * 128
        prv = 128 - cur
        pcb_old = ib128 if j == 0 else Pcb[:, prv:prv + 128]
        qcb_old = None if j == 0 else Qcb[:, prv:prv + 128]
        pcf_old = ident if j == 0 else Pc[:, prv:prv + 128]
        qcf_old = Qc[:, 0:128] if j == 0 else Qc[:, prv:prv + 128]
        T(lambda e, j=j, KT=KT: e.matmul(pS3[0:128, 0:128], x7(j, 1), KT,
                                         start=True, stop=True), glue=True)
        Ss(lambda e, MN=MN: e.mul(MN, pS3[0:128, 0:128], -1.0), glue=True)
        def f_q(e, j=j, qcb_old=qcb_old, KT=KT, MN=MN):
            ins = e.matmul(pS3[0:128, 0:128], KT, x7(j, 0), start=True,
                           stop=(j == 0))
            if j > 0:
                ins = e.matmul(pS3[0:128, 0:128], MN, qcb_old,
                               start=False, stop=True)
            return ins
        T(f_q, glue=True)
        Vs(lambda e, j=j, cur=cur, qcf_old=qcf_old: e.scalar_tensor_tensor(
            Qc[:, cur:cur + 128], qcf_old, clj(CL_GL, j), pS3[0:128, 0:128],
            OP.mult, OP.add), glue=True)
        T(lambda e, pcb_old=pcb_old, MN=MN: e.matmul(pS3[0:128, 0:128], MN, pcb_old,
                                                     start=True, stop=True), glue=True)
        Vs(lambda e, j=j, cur=cur, pcf_old=pcf_old: e.scalar_tensor_tensor(
            Pc[:, cur:cur + 128], pcf_old, clj(CL_GL, j), pS3[0:128, 0:128],
            OP.mult, OP.add), glue=True)
        Vs(lambda e, cur=cur: e.tensor_copy(Qcb[:, cur:cur + 128], Qc[:, cur:cur + 128]),
           glue=True)
        Vs(lambda e, cur=cur: e.tensor_copy(Pcb[:, cur:cur + 128], Pc[:, cur:cur + 128]))
        return st

    def emit_glued(lst, pos):
        while pos < len(lst):
            eng, fn, glue = lst[pos]
            ser.op(eng, fn)
            pos += 1
            if not glue:
                break
        return pos

    for j0 in range(0, NCH, 2):
        A = eager_steps(j0)
        B = eager_steps(j0 + 1)
        ia = ib = 0
        while ia < len(A) or ib < len(B):
            if ia < len(A):
                ia = emit_glued(A, ia)
            if ib < len(B):
                ib = emit_glued(B, ib)

    pq_last = ((NCH - 1) % 2) * 128

    Tn(lambda e: e.transpose(pS1[0:128, 0:128], Pc[:, pq_last:pq_last + 128], ident))
    V(lambda e: e.tensor_copy(combo[:, 0:128], pS1[0:128, 0:128]))
    ser.dma("gpsimd", lambda e: e.dma_start(d_pqloc[0:128, 0:128], combo[:, 0:128]))
    ser.dma("gpsimd", lambda e: e.dma_start(d_pqloc[0:128, 128:256],
                                            Qc[:, pq_last:pq_last + 128]))
    ser.cc("gpsimd", lambda e: e.collective_compute(
        "AllGather", OP.bypass, replica_groups=[list(range(NCORES))],
        ins=[d_pqloc], outs=[d_pqall]))
    for nm2, src2 in (("Pc", Pc), ("Qc", Qc)):
        if nm2 in d_dbg:
            ser.dma("sync", lambda e, nm2=nm2, src2=src2: e.dma_start(
                d_dbg[nm2][0:128, 0:256], src2[:, 0:256]))
    V(lambda e: e.memset(Rf[:, 0:128], 0.0))
    for j in range(NCORES - 1):
        pqs = dgog[:, j * 256:(j + 1) * 256]
        ser.dma("sync" if j % 2 == 0 else "scalar", lambda e, j=j, pqs=pqs:
                e.dma_start(pqs, d_pqall[j * 128:(j + 1) * 128, 0:256]))
        V(lambda e, j=j: e.tensor_scalar(scrB[:, :], ib128,
                                         fb[0:128, FBL.cmi + j:FBL.cmi + j + 1],
                                         None, OP.mult))
        V(lambda e, j=j, pqs=pqs: e.scalar_tensor_tensor(
            pqs[:, 0:128], pqs[:, 0:128], fb[0:128, FBL.cm + j:FBL.cm + j + 1],
            scrB[:, :], OP.mult, OP.add))
        V(lambda e, j=j, pqs=pqs: e.tensor_scalar(
            pqs[:, 128:256], pqs[:, 128:256],
            fb[0:128, FBL.cm + j:FBL.cm + j + 1], None, OP.mult))
    for j in range(NCORES - 1):
        pqs = dgog[:, j * 256:(j + 1) * 256]
        V(lambda e: e.tensor_copy(scrD[:, :], Rf[:, 0:128]))
        Tn(lambda e, pqs=pqs: e.matmul(pS1[0:128, 0:128], pqs[:, 0:128], scrD[:, :],
                                       start=True, stop=True))
        V(lambda e, pqs=pqs: e.scalar_tensor_tensor(Rf[:, 0:128], pS1[0:128, 0:128],
                                                    0.0, pqs[:, 128:256],
                                                    OP.bypass, OP.add))

    for j in range(NCH):
        rcur = (j % 2) * 128
        rnxt = 128 - rcur
        V(lambda e, j=j, rcur=rcur: e.tensor_copy(kslice(sRj, j), Rf[:, rcur:rcur + 128]))
        V(lambda e, j=j: e.tensor_scalar(scrD[:, :], kslice(sKn, j),
                                         clj(CL_KTIL, j), None, OP.mult))
        Tn(lambda e, j=j: e.matmul(pD[0:128, 0:128], kslice(sWf, j), kslice(sRj, j),
                                   start=True, stop=True))
        V(lambda e, j=j: e.scalar_tensor_tensor(scrA[:, :], x7(j, 0), 0.0,
                                                pD[0:128, 0:128], OP.bypass, OP.subtract))
        Tn(lambda e, j=j: e.matmul(pD[0:128, 128:256], scrD[:, :], scrA[:, :],
                                   start=True, stop=True))
        V(lambda e, j=j, rcur=rcur, rnxt=rnxt: e.scalar_tensor_tensor(
            Rf[:, rnxt:rnxt + 128], Rf[:, rcur:rcur + 128], clj(CL_GL, j),
            pD[0:128, 128:256], OP.mult, OP.add))
    def emit_wo_pair(t, pr, alt):
        for k in (0, 1):
            ci = 2 * pr + k
            pW = pD if k == 0 else alt
            Tn(lambda e, t=t, ci=ci, pW=pW: e.matmul(
                pW[0:128, 0:TW], wob[:, ci * 128:(ci + 1) * 128],
                dgog[:, t * TW:(t + 1) * TW], start=True, stop=True))
            V(lambda e, t=t, ci=ci, pW=pW: e.scalar_tensor_tensor(
                xms(ci, t), pW[0:128, 0:TW],
                fb[0:128, FBL.bo + ci:FBL.bo + ci + 1], xms(ci, t), OP.add, OP.add))

    def emit_gates2():
        V(lambda e: e.tensor_copy(t0f[:, T0_X2:T0_X2 + 8],
                                  xm[0:128, 0:FC * TC:TC]))
        ser.dma("sync", lambda e: e.dma_start(d_g2loc[0:128, 0:8],
                                              t0f[:, T0_X2:T0_X2 + 8]))
        ser.cc("gpsimd", lambda e: e.collective_compute(
            "AllGather", OP.bypass, replica_groups=[list(range(NCORES))],
            ins=[d_g2loc], outs=[d_g2all]))
        ser.cc("gpsimd", lambda e: e.collective_compute(
            "AllReduce", OP.add, replica_groups=[list(range(NCORES))],
            ins=[d_bari], outs=[d_baro]))

    for j in range(NCH):
        p = j % 2
        pRQ = pA if p == 0 else pS1
        pS2_ = pB if p == 0 else pS2
        pO = pC if p == 0 else pS3
        cRt = crs[:, (0 + p) * 128:(1 + p) * 128]
        cRQ = crs[:, (2 + p) * 128:(3 + p) * 128]
        cRG = crs[:, (4 + p) * 128:(5 + p) * 128]
        Tn(lambda e, j=j: e.transpose(pT[0:128, 0:128], kslice(sRj, j), ib128))
        Sc(lambda e, cRt=cRt: e.copy(cRt[:, :], pT[0:128, 0:128]))
        Tn(lambda e, j=j, cRt=cRt, pRQ=pRQ: e.matmul(pRQ[0:128, 0:128], cRt[:, :],
                                                     kslice(qf, j), start=True, stop=True))
        Sc(lambda e, cRQ=cRQ, pRQ=pRQ: e.copy(cRQ[:, :], pRQ[0:128, 0:128]))
        Tn(lambda e, j=j, pRQ=pRQ: e.matmul(pRQ[0:128, 0:128], or1b, rlj(RL_G, j),
                                            start=True, stop=True))
        V(lambda e, cRG=cRG, cRQ=cRQ, pRQ=pRQ: e.tensor_tensor(
            cRG[:, :], cRQ[:, :], pRQ[0:128, 0:128], OP.mult))
        Tn(lambda e, j=j, cRQ=cRQ, pS2_=pS2_: e.matmul(pS2_[0:128, 0:128], kslice(sWf, j),
                                                       cRQ[:, :], start=True, stop=True))
        V(lambda e, j=j, cRQ=cRQ, pS2_=pS2_: e.scalar_tensor_tensor(
            cRQ[:, :], x7(j, 2), 0.0, pS2_[0:128, 0:128], OP.bypass, OP.subtract))
        V(lambda e, j=j, cRQ=cRQ: e.tensor_tensor(cRQ[:, :], cRQ[:, :],
                                                  kslice(sE2, j), OP.mult))
        def f_o(e, j=j, cRQ=cRQ, cRG=cRG, pO=pO):
            e.matmul(pO[0:128, 0:128], kslice(sKn, j), cRQ[:, :], start=True, stop=False)
            return e.matmul(pO[0:128, 0:128], ib128, cRG[:, :], start=False, stop=True)
        Tn(f_o)
        V(lambda e, j=j, pO=pO: e.tensor_tensor(kslice(dgog, j), pO[0:128, 0:128],
                                                kslice(gf, j), OP.mult))
        if j >= 3:
            emit_wo_pair((j - 3) // 4, (j - 3) % 4, pS2 if p == 0 else pB)
        if j == 6:
            emit_gates2()
    for pr in (1, 2, 3):
        emit_wo_pair(3, pr, pS2 if pr % 2 == 1 else pB)

    if "og" in d_dbg:
        ser.dma("sync", lambda e: e.dma_start(d_dbg["og"][0:128, 0:TC], dgog[:, 0:TC]))
    for nm, src in (("sX7", sX7), ("sE2", sE2), ("sKn", sKn), ("sWf", sWf),
                    ("cl", cl), ("Rf", Rf)):
        if nm in d_dbg:
            ser.dma("sync", lambda e, nm=nm, src=src: e.dma_start(
                d_dbg[nm][0:128, :], src[:, :]))

    if "x2" in d_dbg:
        for ci in range(FC):
            ser.dma("sync", lambda e, ci=ci: e.dma_start(
                d_dbg["x2"][ci * 128:(ci + 1) * 128, 0:TC], xm[:, ci * TC:(ci + 1) * TC]))

    stats_AD_inline()
    ser.dma("sync", lambda e: e.dma_start(t0f[:, T0_X2:T0_X2 + 8], d_g2all[0:128, 0:8]))
    t0_lnstats(t0f[:, T0_X2:T0_X2 + 8], T0_LN3)
    t0_gates(T0_LN3, 2)
    emit_ffn(3)

    for ci in range(FC):
        ser.dma("sync" if ci % 2 == 0 else "scalar", lambda e, ci=ci: e.dma_start(
            d_y[ci * 128:(ci + 1) * 128, 0:TC], xm[:, ci * TC:(ci + 1) * TC]))

    ser.build(nc, sems, sb_names, ps_names)
    ctx.close()
    return nc


def _host_blobs(inputs, core):
    eye = np.eye(128, dtype=np.float32)
    tri = np.triu(np.ones((128, 128), dtype=np.float32))
    ones = np.ones((128, 128), dtype=np.float32)
    ss, tt = np.meshgrid(np.arange(128), np.arange(128), indexing="ij")
    minfS = np.where(ss < tt, 0.0, NEG).astype(np.float32)
    minfI = np.where(ss <= tt, 0.0, NEG).astype(np.float32)
    x = np.asarray(inputs["L0_attn_x"][0], np.float32)
    x0fm = np.ascontiguousarray(x[0].reshape(8, 128).T)
    bq = np.asarray(inputs["L0_attn_bq"], np.float32).reshape(128, 1)
    bk = np.asarray(inputs["L0_attn_bk"], np.float32).reshape(128, 1)
    bv = np.asarray(inputs["L0_attn_bv"], np.float32).reshape(128, 1)
    bg = np.asarray(inputs["L0_attn_bg"], np.float32).reshape(128, 1)
    bo = np.ascontiguousarray(np.asarray(inputs["L0_attn_bo"], np.float32).reshape(8, 128).T)
    ba = float(np.asarray(inputs["L0_attn_ba"]).reshape(-1)[0])
    bb_ = float(np.asarray(inputs["L0_attn_bb"]).reshape(-1)[0])
    bab = np.empty((128, 2), np.float32)
    bab[:, 0] = ba
    bab[:, 1] = bb_
    cm = np.zeros((128, 8), np.float32)
    cm[:, :core] = 1.0
    WaT = np.asarray(inputs["L0_attn_Wa"], np.float32).T
    WbT = np.asarray(inputs["L0_attn_Wb"], np.float32).T
    wabo = np.zeros((128, 32), np.float32)
    for ci in range(8):
        wabo[:, 4 * ci] = WaT[ci * 128:(ci + 1) * 128, 0]
        wabo[:, 4 * ci + 1] = WbT[ci * 128:(ci + 1) * 128, 0]
        wabo[:, 4 * ci + 2] = 1.0 / 1024.0
    rwT = np.asarray(inputs["L0_ffn_router_w"], np.float32).T
    rw = np.zeros((128, 64), np.float32)
    for ci in range(8):
        rw[:, ci * 8:(ci + 1) * 8] = rwT[ci * 128:(ci + 1) * 128, :]
    oc1024 = np.full((128, 1), 1.0 / 1024.0, np.float32)
    oc1 = np.ones((128, 1), np.float32)
    zero = np.zeros((128, 1), np.float32)
    eps = np.full((128, 1), 1e-5, np.float32)
    cmi = 1.0 - cm
    fblob = np.concatenate([eye, tri, ones, minfS, minfI, x0fm, bq, bk, bv, bg,
                            bo, bab, cm, wabo, rw, oc1024, oc1, zero, eps, cmi],
                           axis=1).astype(np.float32)
    bblob = np.concatenate([eye, oc1, oc1024], axis=1).astype(BFD)
    rblob = np.ones((1, 128), np.float32).astype(BFD)
    eseg = np.zeros((1, 1024), np.float32)
    for ee in range(8):
        eseg[0, ee * 128 + ee * 16:ee * 128 + (ee + 1) * 16] = 1.0
    rfblob = np.concatenate([np.ones((1, 128), np.float32), eseg], axis=1).astype(np.float32)
    return fblob, bblob, rblob, rfblob


_PROG_CACHE = {}


def _get_prog(dbg=()):
    key = tuple(sorted(dbg))
    if key not in _PROG_CACHE:
        _PROG_CACHE[key] = build_program(dbg)
    return _PROG_CACHE[key]


def kernel(dbg=(), **inputs):
    x = np.asarray(inputs["L0_attn_x"])
    assert x.shape == (1, 16384, 1024)
    xf = np.asarray(x[0], np.float32)
    wqT = np.ascontiguousarray(np.asarray(inputs["L0_attn_Wq"], np.float32).T)
    wkT = np.ascontiguousarray(np.asarray(inputs["L0_attn_Wk"], np.float32).T)
    wvT = np.ascontiguousarray(np.asarray(inputs["L0_attn_Wv"], np.float32).T)
    wgT = np.ascontiguousarray(np.asarray(inputs["L0_attn_Wg"], np.float32).T)
    woT = np.ascontiguousarray(np.asarray(inputs["L0_attn_Wo"], np.float32).T)
    ldT = np.ascontiguousarray(
        np.asarray(inputs["L0_ffn_lora_down"], np.float32).reshape(128, 1024).T)
    lu_cat = np.asarray(inputs["L0_ffn_lora_up"], np.float32).transpose(1, 0, 2).reshape(1024, 128)
    luT = np.ascontiguousarray(lu_cat.T)

    in_maps = []
    for c in range(NCORES):
        fblob, bblob, rblob, rfblob = _host_blobs(inputs, c)
        xT = np.ascontiguousarray(xf[c * TC:(c + 1) * TC, :].T)
        in_maps.append(dict(xT=xT, wqT=wqT, wkT=wkT, wvT=wvT, wgT=wgT, woT=woT,
                            ldT=ldT, luT=luT, fblob=fblob, bblob=bblob,
                            rblob=rblob, rfblob=rfblob))

    nc = _get_prog(dbg)
    res = run_bass_kernel_spmd(nc, in_maps, core_ids=list(range(NCORES)))
    out = np.empty((16384, 1024), np.float32)
    for c in range(NCORES):
        out[c * TC:(c + 1) * TC, :] = res.results[c]["y"].T
    if dbg:
        return out[None], res
    return out[None]
